# version 08
# baseline (speedup 1.0000x reference)
import numpy as np
from contextlib import ExitStack

import concourse.bass as bass
import concourse.mybir as mybir

F32 = mybir.dt.float32
BF16 = mybir.dt.bfloat16
U8 = mybir.dt.uint8
AF = mybir.ActivationFunctionType
ALU = mybir.AluOpType
AX = mybir.AxisListType

T = 2048
D = 1024
NCH = 8
H = 8
E0 = 8
E1 = 16
I1 = 512
NEG_M = -45.0
LN_EPS = 1e-5
SQDK = float(1.0 / np.sqrt(128))
SEG = 256
NSEG = T // SEG
KB_ = 1024
SGOFF = [174, 178, 182, 186, 68, 134, 138, 72]


ACT_SET_OF = {}


class ESync:

    def __init__(self, kb, eng, sem):
        self._kb = kb
        self._eng = eng
        self._sem = sem

    def wait_ge(self, sem, val):
        return self._eng.wait_ge(sem, val)

    def __getattr__(self, name):
        fn = getattr(self._eng, name)
        if not callable(fn):
            return fn
        kb, eng, semn = self._kb, self._eng, self._sem

        def wrapped(*a, **k):
            if name == "activation":
                sel = kb.act_map.get(a[2])
                if sel is not None and sel != kb.act_set:
                    import concourse.mybir as _mb
                    eng.add_instruction(_mb.InstLoadActFuncSet(
                        name=kb.nc.get_next_instruction_name(),
                        act_func_set_id=sel))
                    kb.act_set = sel
            i = fn(*a, **k)
            if hasattr(i, "then_inc"):
                kb.cnt[semn] += 1
                i.then_inc(kb.sems[semn], 1)
                eng.wait_ge(kb.sems[semn], kb.cnt[semn])
                kb.last_auto = i
            return i
        return wrapped


class S:
    def __init__(self, nc, stack, block, arena):
        self.nc = nc
        self.block = block
        self.arena = arena
        self.sems = {}
        self.cnt = {}
        self.last_auto = None
        self.act_set = None
        from concourse.hw_specs import get_activation_tables
        _names = list(get_activation_tables(nc.m.arch).keys())
        _idx = {n: i for i, n in enumerate(_names)}
        import concourse.mybir as _mb
        _AF = _mb.ActivationFunctionType
        self.act_map = {
            _AF.Exp: _idx["natural_log_exp_and_others"],
            _AF.Ln: _idx["natural_log_exp_and_others"],
            _AF.Sigmoid: _idx["sigmoid_and_others"],
            _AF.Silu: _idx["silu_and_others"],
            _AF.Sqrt: _idx["sqrt_and_others"],
        }
        for name in ["ld", "wd", "pe", "ve", "se", "sp"]:
            self.sems[name] = stack.enter_context(nc.semaphore(name))
            self.cnt[name] = 0

    def inc(self, instr, sem, n=1):
        if instr is self.last_auto:
            return self.cnt[sem]
        self.cnt[sem] += n
        instr.then_inc(self.sems[sem], n)
        return self.cnt[sem]

    def w(self, eng, sem, val=None):
        v = self.cnt[sem] if val is None else val
        if v > 0:
            eng.wait_ge(self.sems[sem], v)

    def t(self, f): self.block.tensor(f)

    def v(self, f):
        self.block.vector(lambda e: f(ESync(self, e, "ve")))

    def s(self, f):
        self.block.scalar(lambda e: f(ESync(self, e, "se")))
    def g(self, f): self.block.gpsimd(f)
    def y(self, f): self.block.sync(f)

    def av(self, kb_off, dtype, shape):
        esz = 4 if dtype == F32 else 2
        n = int(np.prod(shape)) * esz
        ap = self.arena.ap()[:, kb_off * KB_: kb_off * KB_ + n].bitcast(dtype)
        if len(shape) == 1:
            return ap
        pat = " ".join("abcdefg"[: len(shape)])
        kw = {"abcdefg"[i]: shape[i] for i in range(len(shape) - 1)}
        return ap.rearrange("p (%s) -> p %s" % (pat, pat), **kw)

    def row(self, kb_off, dtype, n, parts=1):
        esz = 4 if dtype == F32 else 2
        return self.arena.ap()[0:parts, kb_off * KB_: kb_off * KB_ + n * esz].bitcast(dtype)


def build(dump=()):
    nc = bass.Bass("TRN2", target_bir_lowering=False, detect_race_conditions=False)
    dump = set(dump)
    dr = {}

    def din(name, shape):
        dr[name] = nc.dram_tensor(name, list(shape), F32, kind="ExternalInput").ap()

    din("x_in", [D, T])
    din("wq", [D, D]); din("wk0", [D, D]); din("wv0", [D, D * H]); din("wg0", [D, E0])
    din("we0", [E0, D, D])
    din("gamma_p", [128, NCH]); din("beta_p", [128, NCH])
    din("bq_p", [128, NCH]); din("bk_p", [128, NCH])
    din("bv_row", [1, D * H]); din("bg_row", [1, E0]); din("be0_sb", [E0, D])
    din("wkT", [D, D]); din("wvT", [D, D]); din("wrT", [D, D]); din("woT", [D, D])
    din("gwT", [D, E1])
    din("we1", [E1, D, I1]); din("we3", [E1, D, I1]); din("we2", [E1, I1, D])
    din("ws1", [D, D]); din("ws3", [D, D]); din("ws2", [D, D])
    din("td_p", [128, NCH]); din("tf_p", [128, NCH])
    din("tmk_p", [128, NCH]); din("tmv_p", [128, NCH]); din("tmr_p", [128, NCH])
    din("gb_row", [1, E1])
    din("be1_p", [128, E1 * 4]); din("be3_p", [128, E1 * 4]); din("be2_sb", [E1, D])
    din("bs1_p", [128, NCH]); din("bs3_p", [128, NCH]); din("bs2_p", [128, NCH])
    din("ones_col_f", [128, 1]); din("onesrow_f", [1, 128]); din("c1024_row", [1, 128])
    din("negm_col", [128, 1]); din("zero_col", [128, 1])
    din("ones_bf", [128, 128]); din("ident_bf", [128, 128])
    din("oh8", [E0, E0 * 128]); din("oh16", [E1, E1 * 128]); din("ident_f", [128, 128])

    out = nc.dram_tensor("out_fm", [D, T], F32, kind="ExternalOutput").ap()
    xspill = nc.dram_tensor("xspill", [D, T], F32).ap()
    out_names = ["out_fm"]

    ctx = ExitStack()
    sb = {}
    arena = ctx.enter_context(nc.sbuf_tensor("arena", [128, 192 * KB_], U8))
    for name, shape, dt in [
        ("gamma_p", [128, NCH], F32), ("beta_p", [128, NCH], F32),
        ("bq_p", [128, NCH], F32), ("bk_p", [128, NCH], F32),
        ("td_p", [128, NCH], F32), ("tf_p", [128, NCH], F32),
        ("tmk_p", [128, NCH], F32), ("tmv_p", [128, NCH], F32), ("tmr_p", [128, NCH], F32),
        ("be1_p", [128, E1 * 4], F32), ("be3_p", [128, E1 * 4], F32),
        ("bs1_p", [128, NCH], F32), ("bs3_p", [128, NCH], F32), ("bs2_p", [128, NCH], F32),
        ("ones_col_f", [128, 1], F32), ("onesrow_f", [1, 128], F32),
        ("c1024_row", [1, 128], F32), ("negm_col", [128, 1], F32), ("zero_col", [128, 1], F32),
        ("ones_bf", [128, 128], BF16), ("ident_bf", [128, 128], BF16),
        ("oh8", [E0, E0 * 128], BF16), ("oh16", [E1, E1 * 128], BF16),
        ("ident_f", [128, 128], F32),
        ("bg_row", [1, E0], F32), ("gb_row", [1, E1], F32),
        ("be0_sb", [E0, D], BF16), ("be2_sb", [E1, D], BF16),
        ("ew", [128, NCH], F32), ("eu", [128, NCH], F32),
        ("tmk1", [128, NCH], F32), ("tmv1", [128, NCH], F32), ("tmr1", [128, NCH], F32),
        ("carA", [128, NCH], F32), ("carB", [128, NCH], F32),
        ("m1", [128, 16], F32), ("m2", [128, 16], F32), ("cnt2", [128, 16], F32),
        ("gm1", [128, 16, 4], F32), ("gm2", [128, 16, 4], F32), ("gcnt", [128, 16, 4], F32),
        ("gbb", [128, E1], F32), ("gs1", [128, 16, 4], F32), ("stm", [128, 64], F32),
    ]:
        sb[name] = ctx.enter_context(nc.sbuf_tensor("sb_" + name, shape, dt))

    block = ctx.enter_context(nc.Block())
    kb = S(nc, ctx, block, arena)

    x = kb.av(0, F32, [NCH, T])
    xn = kb.av(64, BF16, [NCH, T])
    sq = kb.av(96, F32, [2, T])
    ssx = kb.row(112, F32, T)
    rstd = kb.row(120, F32, T)
    r2 = kb.row(128, F32, T)
    cw0 = kb.row(138, BF16, T, parts=E0)
    cw1 = kb.row(138, BF16, T, parts=E1)
    g0 = kb.av(96, BF16, [NCH, T])
    xn2 = kb.av(142, BF16, [NCH, T])
    ssx2 = kb.row(174, F32, T)
    r2b = kb.row(182, F32, T)

    def sgv(k1):
        return kb.av(SGOFF[k1], BF16, [T])

    tmp512 = kb.av(190, F32, [512])
    sb["lntmp"] = sb["htmp"] = type("W", (), {"ap": staticmethod(lambda: tmp512)})()

    def emit_dump(name, src_ap):
        if name not in dump:
            return
        shp = list(src_ap.shape)
        dt = src_ap.dtype
        dshape = [shp[0] * shp[1], shp[2]] if len(shp) == 3 else shp
        da = nc.dram_tensor("dump_" + name, dshape, dt, kind="ExternalOutput").ap()
        out_names.append("dump_" + name)

        @kb.y
        def _(sy):
            kb.w(sy, "ve"); kb.w(sy, "se"); kb.w(sy, "pe")
            if len(shp) == 3:
                kb.inc(sy.dma_start(da.rearrange("(c p) t -> p c t", p=shp[0]), src_ap), "sp", 16)
            else:
                kb.inc(sy.dma_start(da, src_ap), "sp", 16)
            kb.w(sy, "sp")

        @kb.v
        def _(v):
            kb.w(v, "sp")

        @kb.s
        def _(s):
            kb.w(s, "sp")

        @kb.t
        def _(t):
            kb.w(t, "sp")

        @kb.g
        def _(g):
            kb.w(g, "sp")

    @kb.g
    def _(g):
        for name in ["gamma_p", "beta_p", "bq_p", "bk_p", "td_p", "tf_p",
                     "tmk_p", "tmv_p", "tmr_p", "be1_p", "be3_p",
                     "bs1_p", "bs3_p", "bs2_p", "ones_col_f", "onesrow_f",
                     "c1024_row", "negm_col", "zero_col",
                     "ones_bf", "ident_bf", "bg_row", "gb_row", "be0_sb", "be2_sb",
                     "oh8", "oh16", "ident_f"]:
            kb.inc(g.dma_start(sb[name].ap(), dr[name]), "ld", 16)

    @kb.y
    def _(sy):
        kb.inc(sy.dma_start(x, dr["x_in"].rearrange("(c p) t -> p c t", p=128)), "sp", 16)

    @kb.s
    def _(s):
        kb.w(s, "ld")
        s.activation(sb["ew"].ap(), sb["td_p"].ap(), AF.Exp)
        kb.inc(s.activation(sb["eu"].ap(), sb["tf_p"].ap(), AF.Exp), "se")

    @kb.v
    def _(v):
        kb.w(v, "se")
        kb.inc(v.tensor_scalar(sb["ew"].ap(), sb["ew"].ap(), -1.0, None, ALU.mult), "ve")

    @kb.s
    def _(s):
        kb.w(s, "ve")
        kb.inc(s.activation(sb["ew"].ap(), sb["ew"].ap(), AF.Exp), "se")

    @kb.v
    def _(v):
        kb.w(v, "ld")
        v.tensor_scalar(sb["tmk1"].ap(), sb["tmk_p"].ap(), -1.0, 1.0, ALU.mult, ALU.add)
        v.tensor_scalar(sb["tmv1"].ap(), sb["tmv_p"].ap(), -1.0, 1.0, ALU.mult, ALU.add)
        kb.inc(v.tensor_scalar(sb["tmr1"].ap(), sb["tmr_p"].ap(), -1.0, 1.0, ALU.mult, ALU.add), "ve")

    def emit_ln(tag, want_r2):
        st0 = ExitStack()
        ps_stats = st0.enter_context(nc.psum_tensor("ps_st_" + tag, [1, T], F32))

        @kb.t
        def _(t):
            kb.w(t, "ld"); kb.w(t, "ve"); kb.w(t, "sp"); kb.w(t, "se")
            for kc in range(NCH):
                for ti in range(4):
                    i = t.matmul(ps_stats[0:1, 512 * ti:512 * (ti + 1)], sb["ones_col_f"].ap(),
                                 x[:, kc, 512 * ti:512 * (ti + 1)], start=(kc == 0), stop=(kc == NCH - 1))
            kb.inc(i, "pe")

        @kb.v
        def _(v):
            kb.w(v, "pe")
            kb.inc(v.tensor_copy(ssx, ps_stats.ap()), "ve")

        ve_ssx = kb.cnt["ve"]
        pe_sq = []
        for kc in range(NCH):
            @kb.s
            def _(s, kc=kc):
                if kc == 0:
                    kb.w(s, "sp"); kb.w(s, "ld"); kb.w(s, "ve")
                if kc >= 2:
                    kb.w(s, "pe", pe_sq[kc - 2])
                kb.inc(s.activation(sq[:, kc % 2, :], x[:, kc, :], AF.Square), "se")

            @kb.t
            def _(t, kc=kc):
                kb.w(t, "se")
                if kc == 0:
                    kb.w(t, "ve", ve_ssx)
                for ti in range(4):
                    i = t.matmul(ps_stats[0:1, 512 * ti:512 * (ti + 1)], sb["ones_col_f"].ap(),
                                 sq[:, kc % 2, 512 * ti:512 * (ti + 1)], start=(kc == 0), stop=(kc == NCH - 1))
                pe_sq.append(kb.inc(i, "pe"))

        ssq = sq[0:1, 1, 0:T]

        @kb.v
        def _(v):
            kb.w(v, "pe")
            kb.inc(v.tensor_copy(ssq, ps_stats.ap()), "ve")
        st0.close()

        st = ExitStack()
        ps_sm = st.enter_context(nc.psum_tensor("ps_sm_" + tag, [128, 2, 16], F32))
        ps_bc = [st.enter_context(nc.psum_tensor("ps_bc_%s_%d" % (tag, i_), [128, 512], F32)) for i_ in range(4)]
        stm = sb["stm"]
        s_veps = stm.ap()[:, 0:16]
        s_scr = stm.ap()[:, 16:32]
        s_rstd = stm.ap()[:, 32:48]
        s_r2 = stm.ap()[:, 48:64]

        @kb.t
        def _(t):
            kb.w(t, "ve")
            for tc in range(16):
                t.matmul(ps_sm[:, 0, tc:tc + 1], ssx[0:1, 128 * tc:128 * (tc + 1)],
                         sb["ones_col_f"][0:1, 0:1], start=True, stop=True)
            for tc in range(16):
                i = t.matmul(ps_sm[:, 1, tc:tc + 1], ssq[0:1, 128 * tc:128 * (tc + 1)],
                             sb["ones_col_f"][0:1, 0:1], start=True, stop=True)
            kb.inc(i, "pe")

        @kb.v
        def _(v):
            kb.w(v, "pe")
            v.tensor_scalar(s_veps, ps_sm[:, 1, :], 1.0 / D, LN_EPS, ALU.mult, ALU.add)
            v.tensor_copy(s_scr, ps_sm[:, 0, :])
            v.tensor_tensor(s_scr, s_scr, s_scr, ALU.mult)
            v.scalar_tensor_tensor(s_veps, s_scr, -1.0 / (D * D), s_veps, ALU.mult, ALU.add)
            kb.inc(v.reciprocal(s_rstd, s_veps), "ve")

        @kb.s
        def _(s):
            kb.w(s, "ve")
            kb.inc(s.activation(s_rstd, s_rstd, AF.Sqrt), "se")

        @kb.v
        def _(v):
            kb.w(v, "se")
            v.tensor_tensor(s_scr, s_rstd, s_rstd, ALU.mult)
            v.tensor_tensor(s_scr, s_scr, s_veps, ALU.mult)
            v.tensor_scalar(s_scr, s_scr, -0.5, 1.5, ALU.mult, ALU.add)
            kb.inc(v.tensor_tensor(s_rstd, s_rstd, s_scr, ALU.mult), "ve")
            if want_r2:
                v.reciprocal(s_scr, s_veps)
                v.tensor_scalar(s_r2, s_veps, 1.0, -LN_EPS, ALU.mult, ALU.add)
                v.tensor_tensor(s_r2, s_r2, s_scr, ALU.mult)
                kb.inc(v.tensor_scalar(s_r2, s_r2, LN_EPS, None, ALU.add), "ve")
                kb.inc(v.reciprocal(s_scr, s_r2), "ve")

        if want_r2:
            @kb.s
            def _(s):
                kb.w(s, "ve")
                kb.inc(s.activation(s_scr, s_scr, AF.Sqrt), "se")

            @kb.v
            def _(v):
                kb.w(v, "se")
                v.tensor_tensor(s_veps, s_scr, s_scr, ALU.mult)
                v.tensor_tensor(s_veps, s_veps, s_r2, ALU.mult)
                v.tensor_scalar(s_veps, s_veps, -0.5, 1.5, ALU.mult, ALU.add)
                v.tensor_tensor(s_scr, s_scr, s_veps, ALU.mult)
                kb.inc(v.tensor_tensor(s_r2, s_scr, s_rstd, ALU.mult), "ve")

        pe_tr = {}

        @kb.t
        def _(t):
            kb.w(t, "ve")
            for tc in range(16):
                i = t.transpose(ps_bc[tc // 4][0:1, 128 * (tc % 4):128 * (tc % 4 + 1)],
                                s_rstd[:, tc:tc + 1], sb["ident_f"].ap())
            pe_tr[0] = kb.inc(i, "pe")

        @kb.v
        def _(v):
            kb.w(v, "pe", pe_tr[0])
            for i_ in range(4):
                i = v.tensor_copy(rstd[0:1, 512 * i_:512 * (i_ + 1)], ps_bc[i_].ap()[0:1, :])
            kb.inc(i, "ve")

        if want_r2:
            @kb.t
            def _(t):
                kb.w(t, "ve")
                for tc in range(16):
                    i = t.transpose(ps_bc[tc // 4][0:1, 128 * (tc % 4):128 * (tc % 4 + 1)],
                                    s_r2[:, tc:tc + 1], sb["ident_f"].ap())
                pe_tr[1] = kb.inc(i, "pe")

            @kb.v
            def _(v):
                kb.w(v, "pe", pe_tr[1])
                for i_ in range(4):
                    i = v.tensor_copy(r2[0:1, 512 * i_:512 * (i_ + 1)], ps_bc[i_].ap()[0:1, :])
                kb.inc(i, "ve")

        for ti in range(4):
            sl = slice(512 * ti, 512 * (ti + 1))

            @kb.t
            def _(t, ti=ti, sl=sl):
                kb.w(t, "ve")
                t.matmul(ps_bc[2 * (ti % 2)].ap(), sb["c1024_row"].ap(), ssx[0:1, sl], start=True, stop=True)
                kb.inc(t.matmul(ps_bc[2 * (ti % 2) + 1].ap(), sb["onesrow_f"].ap(), rstd[0:1, sl],
                                start=True, stop=True), "pe")

            @kb.v
            def _(v, ti=ti, sl=sl):
                kb.w(v, "pe")
                for kc in range(NCH):
                    v.tensor_tensor(sb["lntmp"].ap(), x[:, kc, sl], ps_bc[2 * (ti % 2)].ap(), ALU.subtract)
                    i = v.tensor_tensor(xn[:, kc, sl], sb["lntmp"].ap(), ps_bc[2 * (ti % 2) + 1].ap(), ALU.mult)
                kb.inc(i, "ve")
        st.close()

    emit_ln("l0", want_r2=True)
    emit_dump("xn0", xn)

    def emit_scores_f32(E, wgname, ps_g, ga_t, st):
        wg_f = kb.av(147, F32, [NCH, E])
        csb = kb.av(148, F32, [E])
        mu_tm = kb.av(149, F32, [16])
        negr_tm = kb.av(150, F32, [16])
        csrow = kb.row(151, F32, E)
        ps_cs = st.enter_context(nc.psum_tensor("ps_cs_" + wgname, [1, E], F32))
        ps_sm = st.enter_context(nc.psum_tensor("ps_sm_" + wgname, [128, 2, 16], F32))
        ps_cb = st.enter_context(nc.psum_tensor("ps_cb_" + wgname, [128, E], F32))

        @kb.g
        def _(g):
            kb.w(g, "pe"); kb.w(g, "ve")
            kb.inc(g.dma_start(wg_f[:, :, :], dr[wgname].rearrange("(c p) e -> p c e", p=128)), "wd", 16)

        @kb.t
        def _(t):
            kb.w(t, "wd"); kb.w(t, "ve"); kb.w(t, "se")
            for kc in range(NCH):
                i = t.matmul(ps_cs.ap(), sb["ones_col_f"].ap(), wg_f[:, kc, :],
                             start=(kc == 0), stop=(kc == NCH - 1))
            kb.inc(i, "pe")

        @kb.v
        def _(v):
            kb.w(v, "pe")
            kb.inc(v.tensor_copy(csrow, ps_cs.ap()), "ve")

        @kb.t
        def _(t):
            kb.w(t, "ve")
            kb.inc(t.matmul(ps_cb.ap(), sb["onesrow_f"].ap(), csrow, start=True, stop=True), "pe")
            for tc in range(16):
                t.matmul(ps_sm[:, 0, tc:tc + 1], ssx[0:1, 128 * tc:128 * (tc + 1)],
                         sb["ones_col_f"][0:1, 0:1], start=True, stop=True)
            for tc in range(16):
                i = t.matmul(ps_sm[:, 1, tc:tc + 1], rstd[0:1, 128 * tc:128 * (tc + 1)],
                             sb["ones_col_f"][0:1, 0:1], start=True, stop=True)
            kb.inc(i, "pe")

        @kb.v
        def _(v):
            kb.w(v, "pe")
            v.tensor_copy(csb[:, :], ps_cb.ap())
            v.tensor_scalar(mu_tm[:, :], ps_sm[:, 0, :], 1.0 / D, None, ALU.mult)
            kb.inc(v.tensor_scalar(negr_tm[:, :], ps_sm[:, 1, :], -1.0, None, ALU.mult), "ve")

        pe_s0 = {}
        for tc in range(16):
            @kb.t
            def _(t, tc=tc):
                for kc in range(NCH):
                    i = t.matmul(ps_g[:, tc, :], x[:, kc, 128 * tc:128 * (tc + 1)], wg_f[:, kc, :],
                                 start=(kc == 0), stop=(kc == NCH - 1))
                pe_s0[tc] = kb.inc(i, "pe")

        @kb.v
        def _(v):
            kb.w(v, "pe", pe_s0[15])
            kb.w(v, "ve")
            for tc in range(16):
                v.scalar_tensor_tensor(ga_t[:, tc, :], csb[:, :], mu_tm[:, tc:tc + 1],
                                       ps_g[:, tc, :], ALU.mult, ALU.subtract)
                i = v.tensor_scalar(ga_t[:, tc, :], ga_t[:, tc, :], negr_tm[:, tc:tc + 1], None, ALU.mult)
            kb.inc(i, "ve")

    def emit_gate0():
        st = ExitStack()
        ga_t = kb.av(143, F32, [16, E0])
        gsc_t = kb.av(144, F32, [16, E0])
        cwtm_t = kb.av(145, BF16, [16, E0])
        ga = type("W", (), {"ap": staticmethod(lambda: ga_t)})()
        gsc = type("W", (), {"ap": staticmethod(lambda: gsc_t)})()
        cwtm = type("W", (), {"ap": staticmethod(lambda: cwtm_t)})()
        ps_g = st.enter_context(nc.psum_tensor("ps_g0", [128, 16, E0], F32))
        ps_t = st.enter_context(nc.psum_tensor("ps_t0", [E0, T], BF16))
        m1 = sb["m1"].ap()
        m2 = sb["m2"].ap()
        cnt2 = sb["cnt2"].ap()
        b3 = [128, 16, E0]
        emit_scores_f32(E0, "wg0", ps_g, ga_t, st)
        emit_dump("ga0s", ga_t)
        ps_bg = st.enter_context(nc.psum_tensor("ps_bg0", [128, E0], F32))

        @kb.t
        def _(t):
            kb.w(t, "ve")
            kb.inc(t.matmul(ps_bg.ap(), sb["onesrow_f"].ap(), sb["bg_row"].ap(), start=True, stop=True), "pe")

        @kb.v
        def _(v):
            kb.w(v, "pe")
            kb.inc(v.tensor_tensor(ga.ap(), ga.ap(),
                                   ps_bg.ap().unsqueeze(1).broadcast_to(b3), ALU.add), "ve")
            kb.w(v, "ve")
            kb.inc(v.reduce_max(m1, ga.ap(), axis=AX.X), "ve")
            kb.w(v, "ve")
            kb.inc(v.tensor_tensor(ga.ap(), ga.ap(), m1.unsqueeze(2).broadcast_to(b3), ALU.subtract), "ve")

        @kb.s
        def _(s):
            kb.w(s, "ve")
            kb.inc(s.activation(ga.ap(), ga.ap(), AF.Exp), "se")

        @kb.v
        def _(v):
            kb.w(v, "se")
            kb.inc(v.reduce_sum(m2, ga.ap(), axis=AX.X), "ve")
            kb.w(v, "ve")
            kb.inc(v.reciprocal(m2, m2), "ve")
            kb.w(v, "ve")
            kb.inc(v.tensor_tensor(ga.ap(), ga.ap(), m2.unsqueeze(2).broadcast_to(b3), ALU.mult), "ve")
        emit_dump("ga0sm", ga_t)

        @kb.v
        def _(v):
            kb.w(v, "ve")
            kb.inc(v.reduce_max(m1, ga.ap(), axis=AX.X), "ve")
            kb.w(v, "ve")
            kb.inc(v.tensor_tensor(gsc.ap(), ga.ap(), m1.unsqueeze(2).broadcast_to(b3), ALU.is_ge), "ve")
            kb.w(v, "ve")
            kb.inc(v.reduce_sum(cnt2, gsc.ap(), axis=AX.X), "ve")
            v.scalar_tensor_tensor(gsc.ap(), gsc.ap(), -1e30, ga.ap(), ALU.mult, ALU.add)
            kb.inc(v.reduce_max(m2, gsc.ap(), axis=AX.X), "ve")
            kb.w(v, "ve")
            v.tensor_scalar(cnt2, cnt2, 1.5, None, ALU.is_ge)
            v.tensor_tensor(m1, m1, m2, ALU.subtract)
            v.tensor_tensor(m1, m1, cnt2, ALU.mult)
            kb.inc(v.tensor_tensor(m2, m2, m1, ALU.add), "ve")
            kb.w(v, "ve")
            kb.inc(v.reduce_max(m1, ga.ap(), axis=AX.X), "ve")
            kb.w(v, "ve")
            v.tensor_tensor(gsc.ap(), ga.ap(), m2.unsqueeze(2).broadcast_to(b3), ALU.is_ge)
            v.tensor_tensor(gsc.ap(), gsc.ap(), ga.ap(), ALU.mult)
            kb.inc(v.tensor_tensor(m1, m1, m2, ALU.add), "ve")
            kb.w(v, "ve")
            kb.inc(v.reciprocal(m1, m1), "ve")
            kb.w(v, "ve")
            kb.inc(v.tensor_tensor(cwtm.ap(), gsc.ap(), m1.unsqueeze(2).broadcast_to(b3), ALU.mult), "ve")
        emit_dump("cwtm0", cwtm_t)
        emit_dump("gsel0", gsc_t)

        @kb.t
        def _(t):
            kb.w(t, "ve")
            for tc in range(16):
                i = t.transpose(ps_t[:, 128 * tc:128 * (tc + 1)], cwtm_t[:, tc, :], sb["ident_bf"].ap())
            kb.inc(i, "pe")

        @kb.v
        def _(v):
            kb.w(v, "pe")
            kb.inc(v.tensor_copy(cw0, ps_t.ap()), "ve")
        st.close()

    emit_gate0()
    emit_dump("cw0", cw0)

    @kb.v
    def _(v):
        kb.w(v, "ve"); kb.w(v, "se"); kb.w(v, "pe")
        v.tensor_copy(ssx2, ssx)
        kb.inc(v.tensor_copy(r2b, r2), "ve")

    def emit_xn2():
        st = ExitStack()
        ps_bc = [st.enter_context(nc.psum_tensor("ps_bc_x2_%d" % i_, [128, 512], F32)) for i_ in range(4)]
        for ti in range(4):
            sl = slice(512 * ti, 512 * (ti + 1))

            @kb.t
            def _(t, ti=ti, sl=sl):
                kb.w(t, "ve")
                t.matmul(ps_bc[2 * (ti % 2)].ap(), sb["c1024_row"].ap(), ssx2[0:1, sl], start=True, stop=True)
                kb.inc(t.matmul(ps_bc[2 * (ti % 2) + 1].ap(), sb["onesrow_f"].ap(), r2b[0:1, sl],
                                start=True, stop=True), "pe")

            @kb.v
            def _(v, ti=ti, sl=sl):
                kb.w(v, "pe")
                for kc in range(NCH):
                    v.tensor_tensor(sb["lntmp"].ap(), x[:, kc, sl], ps_bc[2 * (ti % 2)].ap(), ALU.subtract)
                    v.tensor_tensor(sb["lntmp"].ap(), sb["lntmp"].ap(), ps_bc[2 * (ti % 2) + 1].ap(), ALU.mult)
                    i = v.tensor_scalar(xn2[:, kc, sl], sb["lntmp"].ap(), sb["gamma_p"][:, kc:kc + 1],
                                        sb["beta_p"][:, kc:kc + 1], ALU.mult, ALU.add)
                kb.inc(i, "ve")
        st.close()

    emit_xn2()
    emit_dump("xn2", xn2)

    @kb.y
    def _(sy):
        kb.w(sy, "ve")
        kb.inc(sy.dma_start(xspill.rearrange("(c p) t -> p c t", p=128), x), "sp", 16)

    def emit_moe0():
        we = [kb.av(0, BF16, [NCH, D]), kb.av(16, BF16, [NCH, D]), kb.av(32, BF16, [NCH, D])]
        rhs = [kb.av(48, BF16, [NCH, 512]), kb.av(56, BF16, [NCH, 512])]
        cwb = kb.av(128, BF16, [E0, 512])
        st = ExitStack()
        ps_m = st.enter_context(nc.psum_tensor("ps_m0", [128, 8, 512], F32))
        pe_chain = {}
        ve_rhs = {}
        se_evac = {}
        wd_we = {}
        ve_cwb = {}

        for ti in range(4):
            sl = slice(512 * ti, 512 * (ti + 1))
            pe_cwb = {}
            for e in range(E0):
                @kb.t
                def _(t, e=e, ti=ti, sl=sl):
                    if ti == 0 and e == 0:
                        kb.w(t, "sp")
                        kb.w(t, "ve")
                    if e >= 2:
                        kb.w(t, "ve", ve_cwb[(ti, e - 2)])
                    if e == 0 and ti > 0:
                        kb.w(t, "se", se_evac[ti - 1])
                    pe_cwb[e] = kb.inc(t.matmul(ps_m[:, e % 2, :], sb["oh8"][:, 128 * e:128 * (e + 1)],
                                                cw0[:, sl], start=True, stop=True), "pe")

                @kb.v
                def _(v, e=e, ti=ti):
                    kb.w(v, "pe", pe_cwb[e])
                    ve_cwb[(ti, e)] = kb.inc(v.tensor_copy(cwb[:, e, :], ps_m[:, e % 2, :]), "ve")

            for e in range(E0):
                k = ti * E0 + e

                @kb.g
                def _(g, ti=ti, e=e, k=k):
                    if ti == 0 and e == 0:
                        kb.w(g, "sp")
                    if k >= 3:
                        kb.w(g, "pe", pe_chain[k - 3])
                    wd_we[k] = kb.inc(g.dma_start(we[k % 3][:, :, :],
                                                  dr["we0"][e].rearrange("(c p) m -> p c m", p=128)), "wd", 16)

                @kb.v
                def _(v, ti=ti, e=e, k=k, sl=sl):
                    kb.w(v, "ve", ve_cwb[(ti, e)])
                    if k >= 2:
                        kb.w(v, "pe", pe_chain[k - 2])
                    for kc in range(NCH):
                        i = v.tensor_tensor(rhs[k % 2][:, kc, :], xn[:, kc, sl], cwb[:, e, :], ALU.mult)
                    ve_rhs[k] = kb.inc(i, "ve")

                @kb.t
                def _(t, ti=ti, e=e, k=k, sl=sl):
                    kb.w(t, "ve", ve_rhs[k])
                    kb.w(t, "wd", wd_we[k])
                    if e == 0:
                        kb.w(t, "ve", ve_cwb[(ti, E0 - 1)])
                        if ti > 0:
                            kb.w(t, "se", se_evac[ti - 1])
                    for oc in range(8):
                        for kc in range(NCH):
                            i = t.matmul(ps_m[:, oc, :], we[k % 3][:, kc, 128 * oc:128 * (oc + 1)],
                                         rhs[k % 2][:, kc, :], start=(e == 0 and kc == 0), stop=False)
                    pe_chain[k] = kb.inc(i, "pe")

            @kb.t
            def _(t, ti=ti, sl=sl):
                for oc in range(8):
                    i = t.matmul(ps_m[:, oc, :], sb["be0_sb"][:, 128 * oc:128 * (oc + 1)], cw0[:, sl],
                                 start=False, stop=True)
                pe_chain[ti * E0 + E0 - 1] = kb.inc(i, "pe")

            @kb.s
            def _(s, ti=ti, sl=sl):
                kb.w(s, "pe", pe_chain[ti * E0 + E0 - 1])
                for oc in range(8):
                    i = s.activation(g0[:, oc, sl], ps_m[:, oc, :], AF.Silu)
                se_evac[ti] = kb.inc(i, "se")
        st.close()

    emit_moe0()
    emit_dump("g0", g0)

    v2 = kb.av(0, BF16, [16, D])
    wqv = kb.av(32, BF16, [NCH, D])
    wkv_ = kb.av(48, BF16, [NCH, D])
    wv2 = kb.av(64, BF16, [NCH, D])
    qh = kb.av(80, BF16, [T])
    khb = kb.av(84, BF16, [T])
    pT = kb.av(88, BF16, [2, 1024])
    att2 = kb.av(92, BF16, [1024])
    rd = kb.av(64, F32, [1024])
    bv2 = kb.row(132, BF16, D)
    bvt = kb.row(174, BF16, D * H)
    wvt2 = [kb.av(174, BF16, [8192]), kb.av(0, BF16, [8192])]

    def emit_attn0():
        st = ExitStack()
        ve_red = {}

        @kb.g
        def _(g):
            kb.w(g, "se"); kb.w(g, "pe"); kb.w(g, "ve")
            kb.inc(g.dma_start(wqv[:, :, :], dr["wq"].rearrange("(c p) m -> p c m", p=128)), "wd", 16)
            kb.inc(g.dma_start(wkv_[:, :, :], dr["wk0"].rearrange("(c p) m -> p c m", p=128)), "wd", 16)
            kb.inc(g.dma_start(bvt, dr["bv_row"]), "wd", 16)

        @kb.v
        def _(v):
            kb.w(v, "wd")
            with nc.allow_low_precision(reason="bf16 grouped-sum, fp32 internal accum"):
                kb.inc(v.reduce_sum(bv2, bvt.rearrange("a (m u) -> a m u", u=8), axis=AX.X), "ve")

        ve_bv2 = kb.cnt["ve"]
        wd_wv = {}
        for kc in range(NCH):
            @kb.g
            def _(g, kc=kc):
                if kc == 0:
                    kb.w(g, "ve", ve_bv2)
                if kc >= 2:
                    kb.w(g, "ve", ve_red[kc - 2])
                wd_wv[kc] = kb.inc(g.dma_start(wvt2[kc % 2][:, :], dr["wv0"][128 * kc:128 * (kc + 1), :]), "wd", 16)

            @kb.v
            def _(v, kc=kc):
                kb.w(v, "wd", wd_wv[kc])
                with nc.allow_low_precision(reason="bf16 grouped-sum, fp32 internal accum"):
                    ve_red[kc] = kb.inc(v.reduce_sum(wv2[:, kc, :], wvt2[kc % 2].rearrange("p (m u) -> p m u", u=8),
                                                     axis=AX.X), "ve")

        st_v = ExitStack()
        ps_v2 = [st_v.enter_context(nc.psum_tensor("ps_v2_%d" % i_, [128, 512], F32)) for i_ in range(2)]
        ve_v2 = {}
        for tc in range(16):
            for hf in range(2):
                j = tc * 2 + hf

                @kb.t
                def _(t, tc=tc, hf=hf, j=j):
                    if j == 0:
                        kb.w(t, "ve")
                    if j >= 2:
                        kb.w(t, "ve", ve_v2[j - 2])
                    for kc in range(NCH):
                        t.matmul(ps_v2[j % 2].ap(), xn2[:, kc, 128 * tc:128 * (tc + 1)],
                                 wv2[:, kc, 512 * hf:512 * (hf + 1)], start=(kc == 0), stop=False)
                    kb.inc(t.matmul(ps_v2[j % 2].ap(), sb["ones_bf"][0:1, 0:128],
                                    bv2[0:1, 512 * hf:512 * (hf + 1)], start=False, stop=True), "pe")

                @kb.v
                def _(v, tc=tc, hf=hf, j=j):
                    kb.w(v, "pe")
                    ve_v2[j] = kb.inc(v.tensor_copy(v2[:, tc, 512 * hf:512 * (hf + 1)], ps_v2[j % 2].ap()), "ve")
        st_v.close()

        st_h = ExitStack()
        ps_sc = [st_h.enter_context(nc.psum_tensor("ps_sc_%d" % i_, [128, 1024], F32)) for i_ in range(2)]
        ps_att = st_h.enter_context(nc.psum_tensor("ps_att", [128, 1024], F32))
        ps_den = st_h.enter_context(nc.psum_tensor("ps_den", [128, 1024], F32))

        for h in range(H):
            for wi, (wmat, bias, dst) in enumerate(((wqv, "bq_p", qh), (wkv_, "bk_p", khb))):
                for ti in range(4):
                    pe_p = {}

                    @kb.t
                    def _(t, wmat=wmat, h=h, ti=ti, wi=wi, pe_p=pe_p):
                        kb.w(t, "ve")
                        ps = ps_att[:, 0:512] if ti % 2 == 0 else ps_den[:, 0:512]
                        for kc in range(NCH):
                            i = t.matmul(ps, wmat[:, kc, 128 * h:128 * (h + 1)],
                                         xn2[:, kc, 512 * ti:512 * (ti + 1)], start=(kc == 0), stop=(kc == NCH - 1))
                        pe_p[0] = kb.inc(i, "pe")

                    @kb.v
                    def _(v, bias=bias, dst=dst, h=h, ti=ti, wi=wi, pe_p=pe_p):
                        kb.w(v, "pe", pe_p[0])
                        ps = ps_att[:, 0:512] if ti % 2 == 0 else ps_den[:, 0:512]
                        kb.inc(v.tensor_scalar(dst[:, 512 * ti:512 * (ti + 1)], ps,
                                               sb[bias][:, h:h + 1], None, ALU.add), "ve")

            for sh in range(2):
                pe_sc = {}
                se_exp = {}
                pe_pv = {}

                def emit_scores(tc, h, sh):
                    @kb.t
                    def _(t, h=h, sh=sh, tc=tc):
                        if tc == 0:
                            kb.w(t, "ve")
                        if tc >= 2:
                            kb.w(t, "se", se_exp[tc - 2])
                        for half in range(2):
                            i = t.matmul(ps_sc[tc % 2][:, 512 * half:512 * (half + 1)],
                                         khb[:, 128 * tc:128 * (tc + 1)],
                                         qh[:, 1024 * sh + 512 * half:1024 * sh + 512 * (half + 1)],
                                         start=True, stop=True)
                        pe_sc[tc] = kb.inc(i, "pe")

                def emit_exp(tc):
                    @kb.s
                    def _(s, tc=tc):
                        kb.w(s, "pe", pe_sc[tc])
                        if tc >= 2:
                            kb.w(s, "pe", pe_pv[tc - 2])
                        se_exp[tc] = kb.inc(s.activation(pT[:, tc % 2, :], ps_sc[tc % 2].ap(), AF.Exp,
                                                         bias=sb["negm_col"].ap(), scale=SQDK), "se")

                def emit_pv(tc, h):
                    @kb.t
                    def _(t, h=h, tc=tc):
                        kb.w(t, "se", se_exp[tc])
                        for half in range(2):
                            t.matmul(ps_att[:, 512 * half:512 * (half + 1)], v2[:, tc, 128 * h:128 * (h + 1)],
                                     pT[:, tc % 2, 512 * half:512 * (half + 1)], start=(tc == 0), stop=(tc == 15))
                        for half in range(2):
                            i = t.matmul(ps_den[:, 512 * half:512 * (half + 1)], sb["ones_bf"].ap(),
                                         pT[:, tc % 2, 512 * half:512 * (half + 1)], start=(tc == 0), stop=(tc == 15))
                        pe_pv[tc] = kb.inc(i, "pe")

                for tc in range(16):
                    emit_scores(tc, h, sh)
                    emit_exp(tc)
                    if tc >= 1:
                        emit_pv(tc - 1, h)
                emit_pv(15, h)

                @kb.v
                def _(v, h=h, sh=sh):
                    kb.w(v, "pe")
                    v.reciprocal(rd, ps_den.ap())
                    kb.inc(v.tensor_tensor(att2, ps_att.ap(), rd, ALU.mult), "ve")
                    j0 = 256 * h + 128 * sh
                    for k1 in range(NCH):
                        i = v.tensor_tensor(sgv(k1)[:, j0:j0 + 128], att2[:, k1::8],
                                            g0[:, k1, j0:j0 + 128], ALU.mult)
                    kb.inc(i, "ve")
        st_h.close()
        st.close()

    emit_attn0()

    @kb.y
    def _(sy):
        kb.w(sy, "ve"); kb.w(sy, "pe"); kb.w(sy, "se")
        kb.inc(sy.dma_start(x, xspill.rearrange("(c p) t -> p c t", p=128)), "sp", 16)

    @kb.v
    def _(v):
        kb.w(v, "sp")
        for kc in range(NCH):
            for ti in range(4):
                sl = slice(512 * ti, 512 * (ti + 1))
                i = v.tensor_tensor(x[:, kc, sl], x[:, kc, sl], sgv(kc)[:, sl], ALU.add)
        kb.inc(i, "ve")

    emit_dump("x1", x)

    emit_ln("l1", want_r2=False)
    emit_dump("xn1", xn)

    def emit_gate1():
        st = ExitStack()
        ga_t = kb.av(143, F32, [16, E1])
        gb_t = kb.av(144, F32, [16, E1])
        gsc_t = kb.av(145, F32, [16, E1])
        cwtm_t = kb.av(146, BF16, [16, E1])
        ga = type("W", (), {"ap": staticmethod(lambda: ga_t)})()
        gb = type("W", (), {"ap": staticmethod(lambda: gb_t)})()
        gsc = type("W", (), {"ap": staticmethod(lambda: gsc_t)})()
        cwtm = type("W", (), {"ap": staticmethod(lambda: cwtm_t)})()
        ps_g = st.enter_context(nc.psum_tensor("ps_g1", [128, 16, E1], F32))
        ps_b = st.enter_context(nc.psum_tensor("ps_b1", [128, 16], F32))
        ps_t = st.enter_context(nc.psum_tensor("ps_t1", [E1, T], BF16))
        m1 = sb["m1"].ap()
        m2 = sb["m2"].ap()
        cnt2 = sb["cnt2"].ap()
        gm1, gm2, gcnt, gs = sb["gm1"].ap(), sb["gm2"].ap(), sb["gcnt"].ap(), sb["gs1"].ap()
        b3 = [128, 16, E1]
        b4 = [128, 16, 4, 4]
        b34 = [128, 16, 4]
        gb4 = gb_t.rearrange("p a b -> p a b") if False else gb.ap().rearrange("p t (g u) -> p t g u", u=4)
        g4 = gsc.ap().rearrange("p t (g u) -> p t g u", u=4)

        emit_scores_f32(E1, "gwT", ps_g, ga_t, st)

        @kb.t
        def _(t):
            kb.w(t, "ve")
            kb.inc(t.matmul(ps_b.ap()[:, 0:E1] if E1 <= 16 else ps_b.ap(), sb["onesrow_f"].ap(),
                            sb["gb_row"].ap(), start=True, stop=True), "pe")

        @kb.v
        def _(v):
            kb.w(v, "pe")
            kb.inc(v.tensor_copy(sb["gbb"].ap(), ps_b.ap()[:, 0:E1]), "ve")
            kb.w(v, "ve")
            kb.inc(v.reduce_max(m1, ga.ap(), axis=AX.X), "ve")
            kb.w(v, "ve")
            kb.inc(v.tensor_tensor(ga.ap(), ga.ap(), m1.unsqueeze(2).broadcast_to(b3), ALU.subtract), "ve")

        @kb.s
        def _(s):
            kb.w(s, "ve")
            kb.inc(s.activation(ga.ap(), ga.ap(), AF.Exp), "se")

        @kb.v
        def _(v):
            kb.w(v, "se")
            kb.inc(v.reduce_sum(m2, ga.ap(), axis=AX.X), "ve")
            kb.w(v, "ve")
            kb.inc(v.reciprocal(m2, m2), "ve")
            kb.w(v, "ve")
            kb.inc(v.tensor_tensor(ga.ap(), ga.ap(), m2.unsqueeze(2).broadcast_to(b3), ALU.mult), "ve")
            kb.w(v, "ve")
            kb.inc(v.tensor_tensor(gb.ap(), ga.ap(), sb["gbb"].ap().unsqueeze(1).broadcast_to(b3), ALU.add), "ve")
            kb.w(v, "ve")
            kb.inc(v.reduce_max(gm1, gb4, axis=AX.X), "ve")
            kb.w(v, "ve")
            kb.inc(v.tensor_tensor(g4, gb4, gm1.unsqueeze(3).broadcast_to(b4), ALU.is_ge), "ve")
            kb.w(v, "ve")
            kb.inc(v.reduce_sum(gcnt, g4, axis=AX.X), "ve")
            v.scalar_tensor_tensor(gsc.ap(), gsc.ap(), -1e30, gb.ap(), ALU.mult, ALU.add)
            kb.inc(v.reduce_max(gm2, g4, axis=AX.X), "ve")
            kb.w(v, "ve")
            v.tensor_scalar(gcnt, gcnt, 1.5, None, ALU.is_ge)
            v.tensor_tensor(gm1, gm1, gm2, ALU.subtract)
            v.tensor_tensor(gm1, gm1, gcnt, ALU.mult)
            kb.inc(v.tensor_tensor(gm2, gm2, gm1, ALU.add), "ve")
            kb.w(v, "ve")
            kb.inc(v.reduce_max(gm1, gb4, axis=AX.X), "ve")
            kb.w(v, "ve")
            kb.inc(v.tensor_tensor(gs, gm1, gm2, ALU.add), "ve")
            kb.w(v, "ve")
            kb.inc(v.reduce_max(m1, gs, axis=AX.X), "ve")
            kb.w(v, "ve")
            kb.inc(v.tensor_tensor(gm1, gs, m1.unsqueeze(2).broadcast_to(b34), ALU.is_ge), "ve")
            kb.w(v, "ve")
            kb.inc(v.reduce_sum(cnt2, gm1, axis=AX.X), "ve")
            v.scalar_tensor_tensor(gm1, gm1, -1e30, gs, ALU.mult, ALU.add)
            kb.inc(v.reduce_max(m2, gm1, axis=AX.X), "ve")
            kb.w(v, "ve")
            v.tensor_scalar(cnt2, cnt2, 1.5, None, ALU.is_ge)
            v.tensor_tensor(m1, m1, m2, ALU.subtract)
            v.tensor_tensor(m1, m1, cnt2, ALU.mult)
            kb.inc(v.tensor_tensor(m2, m2, m1, ALU.add), "ve")
            kb.w(v, "ve")
            kb.inc(v.tensor_tensor(gm1, gs, m2.unsqueeze(2).broadcast_to(b34), ALU.is_ge), "ve")
            kb.w(v, "ve")
            kb.inc(v.tensor_scalar(gm1, gm1, 1e30, -1e30, ALU.mult, ALU.add), "ve")
            kb.w(v, "ve")
            kb.inc(v.tensor_tensor(g4, gm1.unsqueeze(3).broadcast_to(b4), gb4, ALU.add), "ve")
            kb.w(v, "ve")
            kb.inc(v.reduce_max(m1, gsc.ap(), axis=AX.X), "ve")
            kb.w(v, "ve")
            kb.inc(v.tensor_tensor(gb.ap(), gsc.ap(), m1.unsqueeze(2).broadcast_to(b3), ALU.is_ge), "ve")
            kb.w(v, "ve")
            kb.inc(v.reduce_sum(cnt2, gb.ap(), axis=AX.X), "ve")
            v.scalar_tensor_tensor(gb.ap(), gb.ap(), -1e30, gsc.ap(), ALU.mult, ALU.add)
            kb.inc(v.reduce_max(m2, gb.ap(), axis=AX.X), "ve")
            kb.w(v, "ve")
            v.tensor_scalar(cnt2, cnt2, 1.5, None, ALU.is_ge)
            v.tensor_tensor(m1, m1, m2, ALU.subtract)
            v.tensor_tensor(m1, m1, cnt2, ALU.mult)
            kb.inc(v.tensor_tensor(m2, m2, m1, ALU.add), "ve")
            kb.w(v, "ve")
            v.tensor_tensor(gb.ap(), gsc.ap(), m2.unsqueeze(2).broadcast_to(b3), ALU.is_ge)
            kb.inc(v.tensor_tensor(cwtm.ap(), gb.ap(), ga.ap(), ALU.mult), "ve")

        @kb.t
        def _(t):
            kb.w(t, "ve")
            for tc in range(16):
                i = t.transpose(ps_t[:, 128 * tc:128 * (tc + 1)], cwtm_t[:, tc, :], sb["ident_bf"].ap())
            kb.inc(i, "pe")

        @kb.v
        def _(v):
            kb.w(v, "pe")
            kb.inc(v.tensor_copy(cw1, ps_t.ap()), "ve")
        st.close()

    emit_gate1()
    emit_dump("cw1", cw1)

    @kb.y
    def _(sy):
        kb.w(sy, "ve")
        kb.inc(sy.dma_start(xspill.rearrange("(c p) t -> p c t", p=128), x), "sp", 16)

    wkT = kb.av(0, BF16, [NCH, D])
    wvT = kb.av(16, BF16, [NCH, D])
    wrT = kb.av(32, BF16, [NCH, D])
    woT = kb.av(48, BF16, [NCH, D])
    mixa = kb.av(96, BF16, [NCH, SEG])
    mixb = kb.av(100, BF16, [NCH, SEG])
    mixc = kb.av(120, BF16, [NCH, SEG])
    rwk = kb.av(124, BF16, [NCH, SEG])
    kbuf = kb.av(104, F32, [NCH, SEG])
    vbuf = kb.av(112, F32, [NCH, SEG])
    srb = kb.av(128, F32, [NCH, SEG])
    Ab = kb.av(142, F32, [NCH, SEG])
    Bb = kb.av(96, F32, [NCH, SEG])
    numb = kb.av(150, F32, [NCH, SEG])
    satt = kb.av(158, BF16, [NCH, T])

    def emit_wkv():
        st = ExitStack()
        ps_p = [st.enter_context(nc.psum_tensor("ps_p8_%d" % i_, [128, SEG], F32)) for i_ in range(2)]
        ps_o = [st.enter_context(nc.psum_tensor("ps_o8_%d" % i_, [128, SEG], F32)) for i_ in range(4)]

        @kb.g
        def _(g):
            kb.w(g, "sp")
            kb.w(g, "ve")
            kb.inc(g.dma_start(wkT[:, :, :], dr["wkT"].rearrange("(c p) m -> p c m", p=128)), "wd", 16)
            kb.inc(g.dma_start(wvT[:, :, :], dr["wvT"].rearrange("(c p) m -> p c m", p=128)), "wd", 16)
            kb.inc(g.dma_start(wrT[:, :, :], dr["wrT"].rearrange("(c p) m -> p c m", p=128)), "wd", 16)
            kb.inc(g.dma_start(woT[:, :, :], dr["woT"].rearrange("(c p) m -> p c m", p=128)), "wd", 16)

        sp_x8 = kb.cnt["sp"]

        def mix_d(v, sgi, t0):
            for kc in range(NCH):
                if sgi == 0:
                    v.tensor_copy(numb[:, kc, 0:1], xn[:, kc, 0:1])
                    i = v.tensor_tensor(numb[:, kc, 1:SEG], xn[:, kc, 1:SEG], xn[:, kc, 0:SEG - 1], ALU.subtract)
                else:
                    i = v.tensor_tensor(numb[:, kc, :], xn[:, kc, t0:t0 + SEG],
                                        xn[:, kc, t0 - 1:t0 + SEG - 1], ALU.subtract)
            return i

        def mixes(v, dst, tmn, tm1n, sgi, t0):
            for kc in range(NCH):
                if sgi == 0:
                    v.tensor_scalar(dst[:, kc, 0:1], xn[:, kc, 0:1], sb[tmn][:, kc:kc + 1], None, ALU.mult)
                    i = v.scalar_tensor_tensor(dst[:, kc, 1:SEG], numb[:, kc, 1:SEG],
                                               sb[tmn][:, kc:kc + 1], xn[:, kc, 0:SEG - 1], ALU.mult, ALU.add)
                else:
                    i = v.scalar_tensor_tensor(dst[:, kc, :], numb[:, kc, :],
                                               sb[tmn][:, kc:kc + 1], xn[:, kc, t0 - 1:t0 + SEG - 1], ALU.mult, ALU.add)
            return i

        state = {"pst": None}

        def emit_wo():
            pst = state["pst"]
            if pst is None:
                return
            t0p = pst["t0"]
            pe_o = {}
            ve_o = {}
            for oc in range(NCH):
                @kb.t
                def _(t, oc=oc):
                    if oc == 0:
                        kb.w(t, "ve", pst["ve_rwk"])
                    if oc >= 4:
                        kb.w(t, "ve", ve_o[oc - 4])
                    for kc in range(NCH):
                        i = t.matmul(ps_o[oc % 4].ap(), woT[:, kc, 128 * oc:128 * (oc + 1)],
                                     rwk[:, kc, :], start=(kc == 0), stop=(kc == NCH - 1))
                    pe_o[oc] = kb.inc(i, "pe")

                @kb.v
                def _(v, oc=oc):
                    kb.w(v, "pe", pe_o[oc])
                    ve_o[oc] = kb.inc(v.tensor_copy(satt[:, oc, t0p:t0p + SEG], ps_o[oc % 4].ap()), "ve")
            pst["ve_o_last"] = ve_o[NCH - 1]
            pst["pe_o_last"] = pe_o[NCH - 1]

        for sgi in range(NSEG):
            t0 = SEG * sgi
            pst = state["pst"]
            cur = {"t0": t0}

            @kb.v
            def _(v, sgi=sgi, t0=t0, pst=pst):
                if sgi == 0:
                    kb.w(v, "sp", sp_x8)
                    kb.w(v, "pe")
                else:
                    kb.w(v, "pe", pst["pe_r_last"])
                mix_d(v, sgi, t0)
                mixes(v, mixa, "tmk_p", "tmk1", sgi, t0)
                mixes(v, mixb, "tmv_p", "tmv1", sgi, t0)
                kb.inc(mixes(v, mixc, "tmr_p", "tmr1", sgi, t0), "ve")

            ve_mix = kb.cnt["ve"]
            emit_wo()

            pe_pr = {}
            ve_ev = {}
            for pi, (wnm, srcb, dstb) in enumerate(((wkT, mixa, kbuf), (wvT, mixb, vbuf))):
                for oc in range(NCH):
                    j = pi * NCH + oc

                    @kb.t
                    def _(t, wnm=wnm, srcb=srcb, oc=oc, j=j, sgi=sgi, pst=pst):
                        if j == 0:
                            kb.w(t, "ve", ve_mix)
                            if sgi == 0:
                                kb.w(t, "wd")
                            else:
                                kb.w(t, "se", pst["se_ek"])
                                kb.w(t, "ve", pst["ve_scan"])
                        if j >= 2:
                            kb.w(t, "se", ve_ev[j - 2])
                        elif sgi > 0:
                            kb.w(t, "se", pst["se_r_last"])
                        for kc in range(NCH):
                            i = t.matmul(ps_p[j % 2].ap(), wnm[:, kc, 128 * oc:128 * (oc + 1)],
                                         srcb[:, kc, :], start=(kc == 0), stop=(kc == NCH - 1))
                        pe_pr[j] = kb.inc(i, "pe")

                    @kb.s
                    def _(s, dstb=dstb, oc=oc, j=j):
                        kb.w(s, "pe", pe_pr[j])
                        ve_ev[j] = kb.inc(s.activation(dstb[:, oc, :], ps_p[j % 2].ap(), AF.Copy), "se")

            pe_r = {}
            se_r = {}
            for oc in range(NCH):
                @kb.t
                def _(t, oc=oc):
                    if oc < 2:
                        kb.w(t, "se", ve_ev[2 * NCH - 1])
                    if oc >= 2:
                        kb.w(t, "se", se_r[oc - 2])
                    for kc in range(NCH):
                        i = t.matmul(ps_p[oc % 2].ap(), wrT[:, kc, 128 * oc:128 * (oc + 1)],
                                     mixc[:, kc, :], start=(kc == 0), stop=(kc == NCH - 1))
                    pe_r[oc] = kb.inc(i, "pe")

                @kb.s
                def _(s, oc=oc):
                    kb.w(s, "pe", pe_r[oc])
                    se_r[oc] = kb.inc(s.activation(srb[:, oc, :], ps_p[oc % 2].ap(), AF.Sigmoid), "se")
            cur["pe_r_last"] = pe_r[NCH - 1]
            cur["se_r_last"] = se_r[NCH - 1]

            @kb.s
            def _(s):
                cur["se_ek"] = kb.inc(s.activation(kbuf[:, :, :], kbuf[:, :, :], AF.Exp), "se")

            @kb.v
            def _(v, sgi=sgi, cur=cur):
                kb.w(v, "se", cur["se_ek"])
                kb.inc(v.tensor_tensor(vbuf[:, :, :], kbuf[:, :, :], vbuf[:, :, :], ALU.mult), "ve")
                for kc in range(NCH):
                    ini = 0.0 if sgi == 0 else sb["carA"][:, kc:kc + 1]
                    v.tensor_tensor_scan(Ab[:, kc, :], sb["ew"][:, kc:kc + 1].broadcast_to([128, SEG]),
                                         vbuf[:, kc, :], ini, ALU.mult, ALU.add)
                for kc in range(NCH):
                    ini = 0.0 if sgi == 0 else sb["carB"][:, kc:kc + 1]
                    i = v.tensor_tensor_scan(Bb[:, kc, :], sb["ew"][:, kc:kc + 1].broadcast_to([128, SEG]),
                                             kbuf[:, kc, :], ini, ALU.mult, ALU.add)
                kb.inc(i, "ve")
                for kc in range(NCH):
                    a0 = sb["zero_col"].ap() if sgi == 0 else sb["carA"][:, kc:kc + 1]
                    b0 = sb["zero_col"].ap() if sgi == 0 else sb["carB"][:, kc:kc + 1]
                    v.scalar_tensor_tensor(numb[:, kc, 0:1], vbuf[:, kc, 0:1], sb["eu"][:, kc:kc + 1],
                                           a0, ALU.mult, ALU.add)
                    v.scalar_tensor_tensor(numb[:, kc, 1:SEG], vbuf[:, kc, 1:SEG], sb["eu"][:, kc:kc + 1],
                                           Ab[:, kc, 0:SEG - 1], ALU.mult, ALU.add)
                    v.scalar_tensor_tensor(vbuf[:, kc, 0:1], kbuf[:, kc, 0:1], sb["eu"][:, kc:kc + 1],
                                           b0, ALU.mult, ALU.add)
                    i = v.scalar_tensor_tensor(vbuf[:, kc, 1:SEG], kbuf[:, kc, 1:SEG], sb["eu"][:, kc:kc + 1],
                                               Bb[:, kc, 0:SEG - 1], ALU.mult, ALU.add)
                v.tensor_copy(sb["carA"].ap(), Ab[:, :, SEG - 1])
                cur["ve_scan"] = kb.inc(v.tensor_copy(sb["carB"].ap(), Bb[:, :, SEG - 1]), "ve")

            @kb.s
            def _(s, cur=cur):
                kb.w(s, "ve", cur["ve_scan"])
                s.activation(vbuf[:, :, :], vbuf[:, :, :], AF.Ln)
                kb.inc(s.activation(vbuf[:, :, :], vbuf[:, :, :], AF.Exp, scale=-1.0), "se")

            @kb.v
            def _(v, sgi=sgi, cur=cur, pst=pst):
                kb.w(v, "se")
                if sgi > 0:
                    kb.w(v, "ve", pst.get("ve_o_last", 0))
                v.tensor_tensor(numb[:, :, :], numb[:, :, :], vbuf[:, :, :], ALU.mult)
                cur["ve_rwk"] = kb.inc(v.tensor_tensor(rwk[:, :, :], numb[:, :, :], srb[:, :, :], ALU.mult), "ve")

            state["pst"] = cur

        emit_wo()
        st.close()

    emit_wkv()
    emit_dump("satt", satt)

    y_acc = kb.av(0, F32, [NCH, 1024])
    we1b = kb.av(32, BF16, [NCH, I1])
    we3b = kb.av(40, BF16, [NCH, I1])
    we2b = [kb.av(48, BF16, [4, D]), kb.av(56, BF16, [4, D])]
    h1b = kb.av(96, BF16, [4, 1024])
    hsc = [kb.av(104, BF16, [4, 1024]), kb.av(112, BF16, [4, 1024])]
    cwbt = kb.av(120, F32, [2, 512])
    wsb = kb.av(104, BF16, [NCH, D])
    shh = kb.av(48, BF16, [NCH, 1024])

    def emit_ffn1():
        st = ExitStack()
        ps_h = [st.enter_context(nc.psum_tensor("ps_h_%d" % i_, [128, 512], F32)) for i_ in range(2)]
        ps_y = [st.enter_context(nc.psum_tensor("ps_y_%d" % i_, [128, 512], F32)) for i_ in range(2)]
        ps_cw = st.enter_context(nc.psum_tensor("ps_cw9", [128, 512], F32))
        NE = 2
        for hf in range(2):
            th0 = 1024 * hf

            wd_s = {}

            @kb.g
            def _(g, hf=hf):
                kb.w(g, "pe")
                kb.w(g, "ve")
                wd_s[1] = kb.inc(g.dma_start(wsb[:, :, :], dr["ws1"].rearrange("(c p) m -> p c m", p=128)), "wd", 16)

            pe_s = {}
            se_s = {}
            for ic in range(NCH):
                for tix in range(2):
                    j = ic * 2 + tix
                    slg = slice(th0 + 512 * tix, th0 + 512 * (tix + 1))

                    @kb.t
                    def _(t, ic=ic, tix=tix, j=j, slg=slg):
                        if j == 0:
                            kb.w(t, "wd", wd_s[1])
                            kb.w(t, "se")
                            kb.w(t, "ve")
                        if j >= 2:
                            kb.w(t, "se", se_s[j - 2])
                        for kc in range(NCH):
                            i = t.matmul(ps_h[j % 2].ap(), wsb[:, kc, 128 * ic:128 * (ic + 1)],
                                         xn[:, kc, slg], start=(kc == 0), stop=(kc == NCH - 1))
                        pe_s[j] = kb.inc(i, "pe")

                    @kb.s
                    def _(s, ic=ic, tix=tix, j=j):
                        kb.w(s, "pe", pe_s[j])
                        se_s[j] = kb.inc(s.activation(shh[:, ic, 512 * tix:512 * (tix + 1)], ps_h[j % 2].ap(),
                                                      AF.Silu, bias=sb["bs1_p"][:, ic:ic + 1]), "se")

            @kb.g
            def _(g):
                kb.w(g, "pe")
                wd_s[3] = kb.inc(g.dma_start(wsb[:, :, :], dr["ws3"].rearrange("(c p) m -> p c m", p=128)), "wd", 16)

            pe_s3 = {}
            ve_s3 = {}
            for ic in range(NCH):
                for tix in range(2):
                    j = ic * 2 + tix
                    slg = slice(th0 + 512 * tix, th0 + 512 * (tix + 1))

                    @kb.t
                    def _(t, ic=ic, tix=tix, j=j, slg=slg):
                        if j == 0:
                            kb.w(t, "wd", wd_s[3])
                            kb.w(t, "se", se_s[2 * NCH - 1])
                        if j >= 2:
                            kb.w(t, "ve", ve_s3[j - 2])
                        for kc in range(NCH):
                            i = t.matmul(ps_h[j % 2].ap(), wsb[:, kc, 128 * ic:128 * (ic + 1)],
                                         xn[:, kc, slg], start=(kc == 0), stop=(kc == NCH - 1))
                        pe_s3[j] = kb.inc(i, "pe")

                    @kb.v
                    def _(v, ic=ic, tix=tix, j=j):
                        kb.w(v, "pe", pe_s3[j])
                        v.tensor_scalar(sb["htmp"].ap(), ps_h[j % 2].ap(), sb["bs3_p"][:, ic:ic + 1], None, ALU.add)
                        ve_s3[j] = kb.inc(v.tensor_tensor(shh[:, ic, 512 * tix:512 * (tix + 1)],
                                                          shh[:, ic, 512 * tix:512 * (tix + 1)],
                                                          sb["htmp"].ap(), ALU.mult), "ve")

            @kb.g
            def _(g):
                kb.w(g, "pe")
                wd_s[2] = kb.inc(g.dma_start(wsb[:, :, :], dr["ws2"].rearrange("(c p) m -> p c m", p=128)), "wd", 16)

            pe_s2 = {}
            ve_s2 = {}
            for dc in range(NCH):
                for tix in range(2):
                    j = dc * 2 + tix
                    sl = slice(512 * tix, 512 * (tix + 1))

                    @kb.t
                    def _(t, dc=dc, tix=tix, j=j):
                        if j == 0:
                            kb.w(t, "wd", wd_s[2])
                            kb.w(t, "ve")
                        if j >= 2:
                            kb.w(t, "ve", ve_s2[j - 2])
                        for kc in range(NCH):
                            i = t.matmul(ps_y[j % 2].ap(), wsb[:, kc, 128 * dc:128 * (dc + 1)],
                                         shh[:, kc, sl], start=(kc == 0), stop=(kc == NCH - 1))
                        pe_s2[j] = kb.inc(i, "pe")

                    @kb.v
                    def _(v, dc=dc, j=j, sl=sl):
                        kb.w(v, "pe", pe_s2[j])
                        ve_s2[j] = kb.inc(v.tensor_scalar(y_acc[:, dc, sl], ps_y[j % 2].ap(),
                                                          sb["bs2_p"][:, dc:dc + 1], None, ALU.add), "ve")

            wq9 = {"w1": {}, "w3": {}, "w2": {}}

            def load_w1(e, wait_pe):
                if e >= E1:
                    return
                @kb.g
                def _(g, e=e, wait_pe=wait_pe):
                    kb.w(g, "pe", wait_pe)
                    wq9["w1"][e] = kb.inc(g.dma_start(we1b[:, :, :], dr["we1"][e].rearrange("(c p) m -> p c m", p=128)), "wd", 16)

            def load_w3(e, wait_pe):
                if e >= E1:
                    return
                @kb.g
                def _(g, e=e, wait_pe=wait_pe):
                    kb.w(g, "pe", wait_pe)
                    wq9["w3"][e] = kb.inc(g.dma_start(we3b[:, :, :], dr["we3"][e].rearrange("(c p) m -> p c m", p=128)), "wd", 16)

            def load_w2(e, ei, wait_pe):
                if e >= E1:
                    return
                @kb.g
                def _(g, e=e, ei=ei, wait_pe=wait_pe):
                    kb.w(g, "pe", wait_pe)
                    wq9["w2"][e] = kb.inc(g.dma_start(we2b[ei][:, :, :], dr["we2"][e].rearrange("(c p) m -> p c m", p=128)), "wd", 16)

            load_w1(0, kb.cnt["pe"])
            load_w3(0, kb.cnt["pe"])
            load_w2(0, 0, kb.cnt["pe"])
            load_w2(1, 1, kb.cnt["pe"])

            for grp in range(E1 // NE):
                ve_hsc = {}
                for ei in range(NE):
                    e = grp * NE + ei

                    pe_cwb = {}
                    for tix in range(2):
                        slg = slice(th0 + 512 * tix, th0 + 512 * (tix + 1))

                        @kb.t
                        def _(t, e=e, tix=tix, slg=slg):
                            kb.w(t, "ve")
                            pe_cwb[tix] = kb.inc(t.matmul(ps_cw.ap(), sb["oh16"][:, 128 * e:128 * (e + 1)],
                                                          cw1[:, slg], start=True, stop=True), "pe")

                        @kb.v
                        def _(v, tix=tix):
                            kb.w(v, "pe", pe_cwb[tix])
                            kb.inc(v.tensor_copy(cwbt[:, tix, :], ps_cw.ap()), "ve")

                    ve_cwb = kb.cnt["ve"]
                    pe_h1 = {}
                    se_h1 = {}
                    for ic in range(4):
                        for tix in range(2):
                            j = ic * 2 + tix
                            slg = slice(th0 + 512 * tix, th0 + 512 * (tix + 1))

                            @kb.t
                            def _(t, e=e, ic=ic, tix=tix, j=j, slg=slg):
                                if j == 0:
                                    kb.w(t, "wd", wq9["w1"][e])
                                    kb.w(t, "se")
                                    kb.w(t, "ve")
                                if j >= 2:
                                    kb.w(t, "se", se_h1[j - 2])
                                for kc in range(NCH):
                                    i = t.matmul(ps_h[j % 2].ap(), we1b[:, kc, 128 * ic:128 * (ic + 1)],
                                                 xn[:, kc, slg], start=(kc == 0), stop=(kc == NCH - 1))
                                pe_h1[j] = kb.inc(i, "pe")

                            @kb.s
                            def _(s, e=e, ic=ic, tix=tix, j=j):
                                kb.w(s, "pe", pe_h1[j])
                                se_h1[j] = kb.inc(s.activation(h1b[:, ic, 512 * tix:512 * (tix + 1)],
                                                               ps_h[j % 2].ap(), AF.Silu,
                                                               bias=sb["be1_p"][:, 4 * e + ic:4 * e + ic + 1]), "se")

                    load_w1(e + 1, pe_h1[7])
                    pe_h3 = {}
                    ve_h3 = {}
                    for ic in range(4):
                        for tix in range(2):
                            j = ic * 2 + tix
                            slg = slice(th0 + 512 * tix, th0 + 512 * (tix + 1))

                            @kb.t
                            def _(t, e=e, ic=ic, tix=tix, j=j, slg=slg):
                                if j == 0:
                                    kb.w(t, "wd", wq9["w3"][e])
                                    kb.w(t, "se", se_h1[7])
                                if j >= 2:
                                    kb.w(t, "ve", ve_h3[j - 2])
                                for kc in range(NCH):
                                    i = t.matmul(ps_h[j % 2].ap(), we3b[:, kc, 128 * ic:128 * (ic + 1)],
                                                 xn[:, kc, slg], start=(kc == 0), stop=(kc == NCH - 1))
                                pe_h3[j] = kb.inc(i, "pe")

                            @kb.v
                            def _(v, e=e, ei=ei, ic=ic, tix=tix, j=j):
                                kb.w(v, "pe", pe_h3[j])
                                if j == 0:
                                    kb.w(v, "ve", ve_cwb)
                                kb.w(v, "se", se_h1[j])
                                v.scalar_tensor_tensor(sb["htmp"].ap(), ps_h[j % 2].ap(),
                                                       sb["be3_p"][:, 4 * e + ic:4 * e + ic + 1],
                                                       h1b[:, ic, 512 * tix:512 * (tix + 1)], ALU.add, ALU.mult)
                                ve_h3[j] = kb.inc(v.tensor_tensor(hsc[ei][:, ic, 512 * tix:512 * (tix + 1)],
                                                                  sb["htmp"].ap(), cwbt[:, tix, :], ALU.mult), "ve")

                    load_w3(e + 1, pe_h3[7])

                pe_y = {}
                ve_y = {}
                for dc in range(NCH):
                    for tix in range(2):
                        j = dc * 2 + tix
                        sl = slice(512 * tix, 512 * (tix + 1))

                        @kb.t
                        def _(t, grp=grp, dc=dc, tix=tix, j=j):
                            if j == 0:
                                kb.w(t, "ve")
                                kb.w(t, "wd", max(wq9["w2"][grp * NE], wq9["w2"][grp * NE + 1]))
                            if j >= 2:
                                kb.w(t, "ve", ve_y[j - 2])
                            for ei in range(NE):
                                for ic in range(4):
                                    i = t.matmul(ps_y[j % 2].ap(), we2b[ei][:, ic, 128 * dc:128 * (dc + 1)],
                                                 hsc[ei][:, ic, 512 * tix:512 * (tix + 1)],
                                                 start=(ei == 0 and ic == 0), stop=(ei == NE - 1 and ic == 3))
                            pe_y[j] = kb.inc(i, "pe")

                        @kb.v
                        def _(v, dc=dc, tix=tix, j=j, sl=sl):
                            kb.w(v, "pe", pe_y[j])
                            ve_y[j] = kb.inc(v.tensor_tensor(y_acc[:, dc, sl], y_acc[:, dc, sl],
                                                             ps_y[j % 2].ap(), ALU.add), "ve")

                load_w2(grp * NE + 2, 0, pe_y[15])
                load_w2(grp * NE + 3, 1, pe_y[15])

            pe_b2 = {}
            for dc in range(NCH):
                for tix in range(2):
                    j = dc * 2 + tix
                    sl = slice(512 * tix, 512 * (tix + 1))
                    slg = slice(th0 + 512 * tix, th0 + 512 * (tix + 1))

                    @kb.t
                    def _(t, dc=dc, tix=tix, j=j, slg=slg):
                        kb.w(t, "ve")
                        pe_b2[j] = kb.inc(t.matmul(ps_y[j % 2].ap(), sb["be2_sb"][:, 128 * dc:128 * (dc + 1)],
                                                   cw1[:, slg], start=True, stop=True), "pe")

                    @kb.v
                    def _(v, dc=dc, j=j, sl=sl):
                        kb.w(v, "pe", pe_b2[j])
                        kb.inc(v.tensor_tensor(y_acc[:, dc, sl], y_acc[:, dc, sl], ps_y[j % 2].ap(), ALU.add), "ve")

            se_g = {}
            for dc in range(NCH):
                for tix in range(2):
                    j = dc * 2 + tix
                    sl = slice(512 * tix, 512 * (tix + 1))
                    slg = slice(th0 + 512 * tix, th0 + 512 * (tix + 1))

                    @kb.s
                    def _(s, dc=dc, j=j, sl=sl):
                        kb.w(s, "ve")
                        se_g[j] = kb.inc(s.activation(h1b[:, j % 4, 0:512], y_acc[:, dc, sl], AF.Silu), "se")

                    @kb.v
                    def _(v, dc=dc, j=j, slg=slg):
                        kb.w(v, "se", se_g[j])
                        kb.inc(v.tensor_tensor(satt[:, dc, slg], satt[:, dc, slg],
                                               h1b[:, j % 4, 0:512], ALU.mult), "ve")
        st.close()

    emit_ffn1()

    @kb.y
    def _(sy):
        kb.w(sy, "ve"); kb.w(sy, "pe"); kb.w(sy, "se")
        kb.inc(sy.dma_start(x, xspill.rearrange("(c p) t -> p c t", p=128)), "sp", 16)

    @kb.v
    def _(v):
        kb.w(v, "sp")
        for kc in range(NCH):
            for ti in range(4):
                sl = slice(512 * ti, 512 * (ti + 1))
                i = v.tensor_tensor(x[:, kc, sl], x[:, kc, sl], satt[:, kc, sl], ALU.add)
        kb.inc(i, "ve")

    @kb.y
    def _(sy):
        kb.w(sy, "ve")
        kb.inc(sy.dma_start(out.rearrange("(c p) t -> p c t", p=128), x), "sp", 16)
        kb.w(sy, "sp")

    ctx.close()
    return nc, out_names


def host_in_map(inputs, b):
    f = np.float32
    A = np.ascontiguousarray

    def pcol(v):
        return A(np.asarray(v, f).reshape(NCH, 128).T)

    m = {}
    m["x_in"] = A(np.asarray(inputs["L0_attn_x"][b], f).T)
    m["wq"] = A(np.asarray(inputs["L0_attn_Wq"], f))
    m["wk0"] = A(np.asarray(inputs["L0_attn_Wk"], f))
    m["wv0"] = A(np.asarray(inputs["L0_attn_Wv"], f))
    m["wg0"] = A(np.asarray(inputs["L0_ffn_Wg"], f))
    m["we0"] = A(np.asarray(inputs["L0_ffn_We"], f))
    m["gamma_p"] = pcol(inputs["L0_attn_gamma"])
    m["beta_p"] = pcol(inputs["L0_attn_beta"])
    m["bq_p"] = pcol(inputs["L0_attn_bq"])
    m["bk_p"] = pcol(inputs["L0_attn_bk"])
    m["bv_row"] = A(np.asarray(inputs["L0_attn_bv"], f)[None, :])
    m["bg_row"] = A(np.asarray(inputs["L0_ffn_bg"], f)[None, :])
    m["be0_sb"] = A(np.asarray(inputs["L0_ffn_be"], f))
    m["wkT"] = A(np.asarray(inputs["L1_attn_Wk"], f).T)
    m["wvT"] = A(np.asarray(inputs["L1_attn_Wv"], f).T)
    m["wrT"] = A(np.asarray(inputs["L1_attn_Wr"], f).T)
    m["woT"] = A(np.asarray(inputs["L1_attn_Wo"], f).T)
    m["gwT"] = A(np.asarray(inputs["L1_ffn_gate_w"], f).T)
    m["we1"] = A(np.asarray(inputs["L1_ffn_We1"], f))
    m["we3"] = A(np.asarray(inputs["L1_ffn_We3"], f))
    m["we2"] = A(np.asarray(inputs["L1_ffn_We2"], f))
    m["ws1"] = A(np.asarray(inputs["L1_ffn_ws1"], f))
    m["ws3"] = A(np.asarray(inputs["L1_ffn_ws3"], f))
    m["ws2"] = A(np.asarray(inputs["L1_ffn_ws2"], f))
    m["td_p"] = pcol(inputs["L1_attn_time_decay"])
    m["tf_p"] = pcol(inputs["L1_attn_time_first"])
    m["tmk_p"] = pcol(np.asarray(inputs["L1_attn_time_mix_k"], f).reshape(-1))
    m["tmv_p"] = pcol(np.asarray(inputs["L1_attn_time_mix_v"], f).reshape(-1))
    m["tmr_p"] = pcol(np.asarray(inputs["L1_attn_time_mix_r"], f).reshape(-1))
    m["gb_row"] = A(np.asarray(inputs["L1_ffn_gate_b"], f)[None, :])
    be1 = np.asarray(inputs["L1_ffn_be1"], f)
    be3 = np.asarray(inputs["L1_ffn_be3"], f)
    m["be1_p"] = A(be1.reshape(E1, 4, 128).transpose(2, 0, 1).reshape(128, E1 * 4))
    m["be3_p"] = A(be3.reshape(E1, 4, 128).transpose(2, 0, 1).reshape(128, E1 * 4))
    m["be2_sb"] = A(np.asarray(inputs["L1_ffn_be2"], f))
    m["bs1_p"] = pcol(inputs["L1_ffn_bs1"])
    m["bs3_p"] = pcol(inputs["L1_ffn_bs3"])
    m["bs2_p"] = pcol(inputs["L1_ffn_bs2"])
    m["ones_col_f"] = np.ones((128, 1), f)
    m["onesrow_f"] = np.ones((1, 128), f)
    m["c1024_row"] = np.full((1, 128), 1.0 / D, f)
    m["negm_col"] = np.full((128, 1), NEG_M, f)
    m["zero_col"] = np.zeros((128, 1), f)
    m["ones_bf"] = np.ones((128, 128), f)
    m["ident_bf"] = np.eye(128, dtype=f)
    m["oh8"] = A(np.repeat(np.eye(E0, dtype=f), 128, axis=1))
    m["oh16"] = A(np.repeat(np.eye(E1, dtype=f), 128, axis=1))
    m["ident_f"] = np.eye(128, dtype=f)
    return m


_BUILT = {}


def kernel(**inputs):
    from concourse.bass_utils import run_bass_kernel_spmd
    if "nc" not in _BUILT:
        _BUILT["nc"], _ = build()
    nc = _BUILT["nc"]
    in_maps = [host_in_map(inputs, b) for b in range(8)]
    res = run_bass_kernel_spmd(nc, in_maps, core_ids=list(range(8)))
    out = np.stack([np.ascontiguousarray(r["out_fm"].T) for r in res.results], axis=0)
    return out.astype(np.float32)


# version 09
# speedup vs baseline: 1.2210x; 1.2210x over previous
import numpy as np
from contextlib import ExitStack

import concourse.bass as bass
import concourse.mybir as mybir

F32 = mybir.dt.float32
BF16 = mybir.dt.bfloat16
U8 = mybir.dt.uint8
AF = mybir.ActivationFunctionType
ALU = mybir.AluOpType
AX = mybir.AxisListType

T = 2048
D = 1024
NCH = 8
H = 8
E0 = 8
E1 = 16
I1 = 512
NEG_M = -45.0
LN_EPS = 1e-5
SQDK = float(1.0 / np.sqrt(128))
SEG = 256
NSEG = T // SEG
KB_ = 1024
SGOFF = [174, 178, 182, 186, 68, 134, 138, 72]


ACT_SET_OF = {}


class ESync:

    def __init__(self, kb, eng, sem):
        self._kb = kb
        self._eng = eng
        self._sem = sem

    def wait_ge(self, sem, val):
        return self._eng.wait_ge(sem, val)

    def __getattr__(self, name):
        fn = getattr(self._eng, name)
        if not callable(fn):
            return fn
        kb, eng, semn = self._kb, self._eng, self._sem

        def wrapped(*a, **k):
            if name == "activation":
                sel = kb.act_map.get(a[2])
                if sel is not None and sel != kb.act_set:
                    import concourse.mybir as _mb
                    eng.add_instruction(_mb.InstLoadActFuncSet(
                        name=kb.nc.get_next_instruction_name(),
                        act_func_set_id=sel))
                    kb.act_set = sel
            i = fn(*a, **k)
            if hasattr(i, "then_inc"):
                kb.cnt[semn] += 1
                i.then_inc(kb.sems[semn], 1)
                eng.wait_ge(kb.sems[semn], kb.cnt[semn])
                kb.last_auto = i
            return i
        return wrapped


class S:
    def __init__(self, nc, stack, block, arena):
        self.nc = nc
        self.block = block
        self.arena = arena
        self.sems = {}
        self.cnt = {}
        self.last_auto = None
        self.act_set = None
        from concourse.hw_specs import get_activation_tables
        _names = list(get_activation_tables(nc.m.arch).keys())
        _idx = {n: i for i, n in enumerate(_names)}
        import concourse.mybir as _mb
        _AF = _mb.ActivationFunctionType
        self.act_map = {
            _AF.Exp: _idx["natural_log_exp_and_others"],
            _AF.Ln: _idx["natural_log_exp_and_others"],
            _AF.Sigmoid: _idx["sigmoid_and_others"],
            _AF.Silu: _idx["silu_and_others"],
            _AF.Sqrt: _idx["sqrt_and_others"],
        }
        for name in ["ld", "wd", "pe", "ve", "se", "sp"]:
            self.sems[name] = stack.enter_context(nc.semaphore(name))
            self.cnt[name] = 0

    def inc(self, instr, sem, n=1):
        if instr is self.last_auto:
            return self.cnt[sem]
        self.cnt[sem] += n
        instr.then_inc(self.sems[sem], n)
        return self.cnt[sem]

    def w(self, eng, sem, val=None):
        v = self.cnt[sem] if val is None else val
        if v > 0:
            eng.wait_ge(self.sems[sem], v)

    def t(self, f): self.block.tensor(f)

    def v(self, f):
        self.block.vector(lambda e: f(ESync(self, e, "ve")))

    def s(self, f):
        self.block.scalar(lambda e: f(ESync(self, e, "se")))
    def g(self, f): self.block.gpsimd(f)
    def y(self, f): self.block.sync(f)

    def av(self, kb_off, dtype, shape):
        esz = 4 if dtype == F32 else 2
        n = int(np.prod(shape)) * esz
        ap = self.arena.ap()[:, kb_off * KB_: kb_off * KB_ + n].bitcast(dtype)
        if len(shape) == 1:
            return ap
        pat = " ".join("abcdefg"[: len(shape)])
        kw = {"abcdefg"[i]: shape[i] for i in range(len(shape) - 1)}
        return ap.rearrange("p (%s) -> p %s" % (pat, pat), **kw)

    def row(self, kb_off, dtype, n, parts=1):
        esz = 4 if dtype == F32 else 2
        return self.arena.ap()[0:parts, kb_off * KB_: kb_off * KB_ + n * esz].bitcast(dtype)


def build(dump=()):
    nc = bass.Bass("TRN2", target_bir_lowering=False, detect_race_conditions=False)
    dump = set(dump)
    dr = {}

    def din(name, shape):
        dr[name] = nc.dram_tensor(name, list(shape), F32, kind="ExternalInput").ap()

    din("x_in", [D, T])
    din("wq", [D, D]); din("wk0", [D, D]); din("wv0", [D, D * H]); din("wg0", [D, E0])
    din("we0", [E0, D, D])
    din("gamma_p", [128, NCH]); din("beta_p", [128, NCH])
    din("bq_p", [128, NCH]); din("bk_p", [128, NCH])
    din("bv_row", [1, D * H]); din("bg_row", [1, E0]); din("be0_sb", [E0, D])
    din("wkT", [D, D]); din("wvT", [D, D]); din("wrT", [D, D]); din("woT", [D, D])
    din("gwT", [D, E1])
    din("we1", [E1, D, I1]); din("we3", [E1, D, I1]); din("we2", [E1, I1, D])
    din("ws1", [D, D]); din("ws3", [D, D]); din("ws2", [D, D])
    din("td_p", [128, NCH]); din("tf_p", [128, NCH])
    din("tmk_p", [128, NCH]); din("tmv_p", [128, NCH]); din("tmr_p", [128, NCH])
    din("gb_row", [1, E1])
    din("be1_p", [128, E1 * 4]); din("be3_p", [128, E1 * 4]); din("be2_sb", [E1, D])
    din("bs1_p", [128, NCH]); din("bs3_p", [128, NCH]); din("bs2_p", [128, NCH])
    din("ones_col_f", [128, 1]); din("onesrow_f", [1, 128]); din("c1024_row", [1, 128])
    din("negm_col", [128, 1]); din("zero_col", [128, 1])
    din("ones_bf", [128, 128]); din("ident_bf", [128, 128])
    din("oh8", [E0, E0 * 128]); din("oh16", [E1, E1 * 128]); din("ident_f", [128, 128])

    out = nc.dram_tensor("out_fm", [D, T], F32, kind="ExternalOutput").ap()
    xspill = nc.dram_tensor("xspill", [D, T], F32).ap()
    out_names = ["out_fm"]

    ctx = ExitStack()
    sb = {}
    arena = ctx.enter_context(nc.sbuf_tensor("arena", [128, 192 * KB_], U8))
    for name, shape, dt in [
        ("gamma_p", [128, NCH], F32), ("beta_p", [128, NCH], F32),
        ("bq_p", [128, NCH], F32), ("bk_p", [128, NCH], F32),
        ("td_p", [128, NCH], F32), ("tf_p", [128, NCH], F32),
        ("tmk_p", [128, NCH], F32), ("tmv_p", [128, NCH], F32), ("tmr_p", [128, NCH], F32),
        ("be1_p", [128, E1 * 4], F32), ("be3_p", [128, E1 * 4], F32),
        ("bs1_p", [128, NCH], F32), ("bs3_p", [128, NCH], F32), ("bs2_p", [128, NCH], F32),
        ("ones_col_f", [128, 1], F32), ("onesrow_f", [1, 128], F32),
        ("c1024_row", [1, 128], F32), ("negm_col", [128, 1], F32), ("zero_col", [128, 1], F32),
        ("ones_bf", [128, 128], BF16), ("ident_bf", [128, 128], BF16),
        ("oh8", [E0, E0 * 128], BF16), ("oh16", [E1, E1 * 128], BF16),
        ("ident_f", [128, 128], F32),
        ("bg_row", [1, E0], F32), ("gb_row", [1, E1], F32),
        ("be0_sb", [E0, D], BF16), ("be2_sb", [E1, D], BF16),
        ("ew", [128, NCH], F32), ("eu", [128, NCH], F32),
        ("tmk1", [128, NCH], F32), ("tmv1", [128, NCH], F32), ("tmr1", [128, NCH], F32),
        ("carA", [128, NCH], F32), ("carB", [128, NCH], F32),
        ("m1", [128, 16], F32), ("m2", [128, 16], F32), ("cnt2", [128, 16], F32),
        ("gm1", [128, 16, 4], F32), ("gm2", [128, 16, 4], F32), ("gcnt", [128, 16, 4], F32),
        ("gbb", [128, E1], F32), ("gs1", [128, 16, 4], F32), ("stm", [128, 64], F32),
    ]:
        sb[name] = ctx.enter_context(nc.sbuf_tensor("sb_" + name, shape, dt))

    block = ctx.enter_context(nc.Block())
    kb = S(nc, ctx, block, arena)

    x = kb.av(0, F32, [NCH, T])
    xn = kb.av(64, BF16, [NCH, T])
    sq = kb.av(96, F32, [2, T])
    ssx = kb.row(112, F32, T)
    rstd = kb.row(120, F32, T)
    r2 = kb.row(128, F32, T)
    cw0 = kb.row(138, BF16, T, parts=E0)
    cw1 = kb.row(138, BF16, T, parts=E1)
    g0 = kb.av(96, BF16, [NCH, T])
    xn2 = kb.av(142, BF16, [NCH, T])
    ssx2 = kb.row(174, F32, T)
    r2b = kb.row(182, F32, T)

    def sgv(k1):
        return kb.av(SGOFF[k1], BF16, [T])

    tmp512 = kb.av(190, F32, [512])
    sb["lntmp"] = sb["htmp"] = type("W", (), {"ap": staticmethod(lambda: tmp512)})()

    def emit_dump(name, src_ap):
        if name not in dump:
            return
        shp = list(src_ap.shape)
        dt = src_ap.dtype
        dshape = [shp[0] * shp[1], shp[2]] if len(shp) == 3 else shp
        da = nc.dram_tensor("dump_" + name, dshape, dt, kind="ExternalOutput").ap()
        out_names.append("dump_" + name)

        @kb.y
        def _(sy):
            kb.w(sy, "ve"); kb.w(sy, "se"); kb.w(sy, "pe")
            if len(shp) == 3:
                kb.inc(sy.dma_start(da.rearrange("(c p) t -> p c t", p=shp[0]), src_ap), "sp", 16)
            else:
                kb.inc(sy.dma_start(da, src_ap), "sp", 16)
            kb.w(sy, "sp")

        @kb.v
        def _(v):
            kb.w(v, "sp")

        @kb.s
        def _(s):
            kb.w(s, "sp")

        @kb.t
        def _(t):
            kb.w(t, "sp")

        @kb.g
        def _(g):
            kb.w(g, "sp")

    @kb.g
    def _(g):
        for name in ["gamma_p", "beta_p", "bq_p", "bk_p", "td_p", "tf_p",
                     "tmk_p", "tmv_p", "tmr_p", "be1_p", "be3_p",
                     "bs1_p", "bs3_p", "bs2_p", "ones_col_f", "onesrow_f",
                     "c1024_row", "negm_col", "zero_col",
                     "ones_bf", "ident_bf", "bg_row", "gb_row", "be0_sb", "be2_sb",
                     "oh8", "oh16", "ident_f"]:
            kb.inc(g.dma_start(sb[name].ap(), dr[name]), "ld", 16)

    @kb.y
    def _(sy):
        kb.inc(sy.dma_start(x, dr["x_in"].rearrange("(c p) t -> p c t", p=128)), "sp", 16)

    @kb.s
    def _(s):
        kb.w(s, "ld")
        s.activation(sb["ew"].ap(), sb["td_p"].ap(), AF.Exp)
        kb.inc(s.activation(sb["eu"].ap(), sb["tf_p"].ap(), AF.Exp), "se")

    @kb.v
    def _(v):
        kb.w(v, "se")
        kb.inc(v.tensor_scalar(sb["ew"].ap(), sb["ew"].ap(), -1.0, None, ALU.mult), "ve")

    @kb.s
    def _(s):
        kb.w(s, "ve")
        kb.inc(s.activation(sb["ew"].ap(), sb["ew"].ap(), AF.Exp), "se")



    def emit_ln(tag, want_r2):
        st0 = ExitStack()
        ps_stats = st0.enter_context(nc.psum_tensor("ps_st_" + tag, [1, T], F32))

        @kb.t
        def _(t):
            kb.w(t, "ld"); kb.w(t, "ve"); kb.w(t, "sp"); kb.w(t, "se")
            for kc in range(NCH):
                for ti in range(4):
                    i = t.matmul(ps_stats[0:1, 512 * ti:512 * (ti + 1)], sb["ones_col_f"].ap(),
                                 x[:, kc, 512 * ti:512 * (ti + 1)], start=(kc == 0), stop=(kc == NCH - 1))
            kb.inc(i, "pe")

        @kb.v
        def _(v):
            kb.w(v, "pe")
            kb.inc(v.tensor_copy(ssx, ps_stats.ap()), "ve")

        ve_ssx = kb.cnt["ve"]
        pe_sq = []
        for kc in range(NCH):
            @kb.s
            def _(s, kc=kc):
                if kc == 0:
                    kb.w(s, "sp"); kb.w(s, "ld"); kb.w(s, "ve")
                if kc >= 2:
                    kb.w(s, "pe", pe_sq[kc - 2])
                kb.inc(s.activation(sq[:, kc % 2, :], x[:, kc, :], AF.Square), "se")

            @kb.t
            def _(t, kc=kc):
                kb.w(t, "se")
                if kc == 0:
                    kb.w(t, "ve", ve_ssx)
                for ti in range(4):
                    i = t.matmul(ps_stats[0:1, 512 * ti:512 * (ti + 1)], sb["ones_col_f"].ap(),
                                 sq[:, kc % 2, 512 * ti:512 * (ti + 1)], start=(kc == 0), stop=(kc == NCH - 1))
                pe_sq.append(kb.inc(i, "pe"))

        ssq = sq[0:1, 1, 0:T]

        @kb.v
        def _(v):
            kb.w(v, "pe")
            kb.inc(v.tensor_copy(ssq, ps_stats.ap()), "ve")
        st0.close()

        st = ExitStack()
        ps_sm = st.enter_context(nc.psum_tensor("ps_sm_" + tag, [128, 2, 16], F32))
        ps_bc = [st.enter_context(nc.psum_tensor("ps_bc_%s_%d" % (tag, i_), [128, 512], F32)) for i_ in range(4)]
        stm = sb["stm"]
        s_veps = stm.ap()[:, 0:16]
        s_scr = stm.ap()[:, 16:32]
        s_rstd = stm.ap()[:, 32:48]
        s_r2 = stm.ap()[:, 48:64]

        @kb.t
        def _(t):
            kb.w(t, "ve")
            for tc in range(16):
                t.matmul(ps_sm[:, 0, tc:tc + 1], ssx[0:1, 128 * tc:128 * (tc + 1)],
                         sb["ones_col_f"][0:1, 0:1], start=True, stop=True)
            for tc in range(16):
                i = t.matmul(ps_sm[:, 1, tc:tc + 1], ssq[0:1, 128 * tc:128 * (tc + 1)],
                             sb["ones_col_f"][0:1, 0:1], start=True, stop=True)
            kb.inc(i, "pe")

        @kb.v
        def _(v):
            kb.w(v, "pe")
            v.tensor_scalar(s_veps, ps_sm[:, 1, :], 1.0 / D, LN_EPS, ALU.mult, ALU.add)
            v.tensor_copy(s_scr, ps_sm[:, 0, :])
            v.tensor_tensor(s_scr, s_scr, s_scr, ALU.mult)
            v.scalar_tensor_tensor(s_veps, s_scr, -1.0 / (D * D), s_veps, ALU.mult, ALU.add)
            kb.inc(v.reciprocal(s_rstd, s_veps), "ve")

        @kb.s
        def _(s):
            kb.w(s, "ve")
            kb.inc(s.activation(s_rstd, s_rstd, AF.Sqrt), "se")

        @kb.v
        def _(v):
            kb.w(v, "se")
            v.tensor_tensor(s_scr, s_rstd, s_rstd, ALU.mult)
            v.tensor_tensor(s_scr, s_scr, s_veps, ALU.mult)
            v.tensor_scalar(s_scr, s_scr, -0.5, 1.5, ALU.mult, ALU.add)
            kb.inc(v.tensor_tensor(s_rstd, s_rstd, s_scr, ALU.mult), "ve")
            if want_r2:
                v.reciprocal(s_scr, s_veps)
                v.tensor_scalar(s_r2, s_veps, 1.0, -LN_EPS, ALU.mult, ALU.add)
                v.tensor_tensor(s_r2, s_r2, s_scr, ALU.mult)
                kb.inc(v.tensor_scalar(s_r2, s_r2, LN_EPS, None, ALU.add), "ve")
                kb.inc(v.reciprocal(s_scr, s_r2), "ve")

        if want_r2:
            @kb.s
            def _(s):
                kb.w(s, "ve")
                kb.inc(s.activation(s_scr, s_scr, AF.Sqrt), "se")

            @kb.v
            def _(v):
                kb.w(v, "se")
                v.tensor_tensor(s_veps, s_scr, s_scr, ALU.mult)
                v.tensor_tensor(s_veps, s_veps, s_r2, ALU.mult)
                v.tensor_scalar(s_veps, s_veps, -0.5, 1.5, ALU.mult, ALU.add)
                v.tensor_tensor(s_scr, s_scr, s_veps, ALU.mult)
                kb.inc(v.tensor_tensor(s_r2, s_scr, s_rstd, ALU.mult), "ve")

        pe_tr = {}

        @kb.t
        def _(t):
            kb.w(t, "ve")
            for tc in range(16):
                i = t.transpose(ps_bc[tc // 4][0:1, 128 * (tc % 4):128 * (tc % 4 + 1)],
                                s_rstd[:, tc:tc + 1], sb["ident_f"].ap())
            pe_tr[0] = kb.inc(i, "pe")

        @kb.v
        def _(v):
            kb.w(v, "pe", pe_tr[0])
            for i_ in range(4):
                i = v.tensor_copy(rstd[0:1, 512 * i_:512 * (i_ + 1)], ps_bc[i_].ap()[0:1, :])
            kb.inc(i, "ve")

        if want_r2:
            @kb.t
            def _(t):
                kb.w(t, "ve")
                for tc in range(16):
                    i = t.transpose(ps_bc[tc // 4][0:1, 128 * (tc % 4):128 * (tc % 4 + 1)],
                                    s_r2[:, tc:tc + 1], sb["ident_f"].ap())
                pe_tr[1] = kb.inc(i, "pe")

            @kb.v
            def _(v):
                kb.w(v, "pe", pe_tr[1])
                for i_ in range(4):
                    i = v.tensor_copy(r2[0:1, 512 * i_:512 * (i_ + 1)], ps_bc[i_].ap()[0:1, :])
                kb.inc(i, "ve")

        for ti in range(4):
            sl = slice(512 * ti, 512 * (ti + 1))

            @kb.t
            def _(t, ti=ti, sl=sl):
                kb.w(t, "ve")
                t.matmul(ps_bc[2 * (ti % 2)].ap(), sb["c1024_row"].ap(), ssx[0:1, sl], start=True, stop=True)
                kb.inc(t.matmul(ps_bc[2 * (ti % 2) + 1].ap(), sb["onesrow_f"].ap(), rstd[0:1, sl],
                                start=True, stop=True), "pe")

            @kb.v
            def _(v, ti=ti, sl=sl):
                kb.w(v, "pe")
                for kc in range(NCH):
                    v.tensor_tensor(sb["lntmp"].ap(), x[:, kc, sl], ps_bc[2 * (ti % 2)].ap(), ALU.subtract)
                    i = v.tensor_tensor(xn[:, kc, sl], sb["lntmp"].ap(), ps_bc[2 * (ti % 2) + 1].ap(), ALU.mult)
                kb.inc(i, "ve")
        st.close()

    emit_ln("l0", want_r2=True)
    emit_dump("xn0", xn)

    def emit_scores_f32(E, wgname, ps_g, ga_t, st):
        wg_f = kb.av(147, F32, [NCH, E])
        csb = kb.av(148, F32, [E])
        mu_tm = kb.av(149, F32, [16])
        negr_tm = kb.av(150, F32, [16])
        csrow = kb.row(151, F32, E)
        ps_cs = st.enter_context(nc.psum_tensor("ps_cs_" + wgname, [1, E], F32))
        ps_sm = st.enter_context(nc.psum_tensor("ps_sm_" + wgname, [128, 2, 16], F32))
        ps_cb = st.enter_context(nc.psum_tensor("ps_cb_" + wgname, [128, E], F32))

        @kb.g
        def _(g):
            kb.w(g, "pe"); kb.w(g, "ve")
            kb.inc(g.dma_start(wg_f[:, :, :], dr[wgname].rearrange("(c p) e -> p c e", p=128)), "wd", 16)

        @kb.t
        def _(t):
            kb.w(t, "wd"); kb.w(t, "ve"); kb.w(t, "se")
            for kc in range(NCH):
                i = t.matmul(ps_cs.ap(), sb["ones_col_f"].ap(), wg_f[:, kc, :],
                             start=(kc == 0), stop=(kc == NCH - 1))
            kb.inc(i, "pe")

        @kb.v
        def _(v):
            kb.w(v, "pe")
            kb.inc(v.tensor_copy(csrow, ps_cs.ap()), "ve")

        @kb.t
        def _(t):
            kb.w(t, "ve")
            kb.inc(t.matmul(ps_cb.ap(), sb["onesrow_f"].ap(), csrow, start=True, stop=True), "pe")
            for tc in range(16):
                t.matmul(ps_sm[:, 0, tc:tc + 1], ssx[0:1, 128 * tc:128 * (tc + 1)],
                         sb["ones_col_f"][0:1, 0:1], start=True, stop=True)
            for tc in range(16):
                i = t.matmul(ps_sm[:, 1, tc:tc + 1], rstd[0:1, 128 * tc:128 * (tc + 1)],
                             sb["ones_col_f"][0:1, 0:1], start=True, stop=True)
            kb.inc(i, "pe")

        @kb.v
        def _(v):
            kb.w(v, "pe")
            v.tensor_copy(csb[:, :], ps_cb.ap())
            v.tensor_scalar(mu_tm[:, :], ps_sm[:, 0, :], 1.0 / D, None, ALU.mult)
            kb.inc(v.tensor_scalar(negr_tm[:, :], ps_sm[:, 1, :], -1.0, None, ALU.mult), "ve")

        pe_s0 = {}
        for tc in range(16):
            @kb.t
            def _(t, tc=tc):
                for kc in range(NCH):
                    i = t.matmul(ps_g[:, tc, :], x[:, kc, 128 * tc:128 * (tc + 1)], wg_f[:, kc, :],
                                 start=(kc == 0), stop=(kc == NCH - 1))
                pe_s0[tc] = kb.inc(i, "pe")

        @kb.v
        def _(v):
            kb.w(v, "pe", pe_s0[15])
            kb.w(v, "ve")
            for tc in range(16):
                v.scalar_tensor_tensor(ga_t[:, tc, :], csb[:, :], mu_tm[:, tc:tc + 1],
                                       ps_g[:, tc, :], ALU.mult, ALU.subtract)
                i = v.tensor_scalar(ga_t[:, tc, :], ga_t[:, tc, :], negr_tm[:, tc:tc + 1], None, ALU.mult)
            kb.inc(i, "ve")

    def emit_gate0():
        st = ExitStack()
        ga_t = kb.av(143, F32, [16, E0])
        gsc_t = kb.av(144, F32, [16, E0])
        cwtm_t = kb.av(145, BF16, [16, E0])
        ga = type("W", (), {"ap": staticmethod(lambda: ga_t)})()
        gsc = type("W", (), {"ap": staticmethod(lambda: gsc_t)})()
        cwtm = type("W", (), {"ap": staticmethod(lambda: cwtm_t)})()
        ps_g = st.enter_context(nc.psum_tensor("ps_g0", [128, 16, E0], F32))
        ps_t = st.enter_context(nc.psum_tensor("ps_t0", [E0, T], BF16))
        m1 = sb["m1"].ap()
        m2 = sb["m2"].ap()
        cnt2 = sb["cnt2"].ap()
        b3 = [128, 16, E0]
        emit_scores_f32(E0, "wg0", ps_g, ga_t, st)
        emit_dump("ga0s", ga_t)
        ps_bg = st.enter_context(nc.psum_tensor("ps_bg0", [128, E0], F32))

        @kb.t
        def _(t):
            kb.w(t, "ve")
            kb.inc(t.matmul(ps_bg.ap(), sb["onesrow_f"].ap(), sb["bg_row"].ap(), start=True, stop=True), "pe")

        @kb.v
        def _(v):
            kb.w(v, "pe")
            kb.inc(v.tensor_tensor(ga.ap(), ga.ap(),
                                   ps_bg.ap().unsqueeze(1).broadcast_to(b3), ALU.add), "ve")
            kb.w(v, "ve")
            kb.inc(v.reduce_max(m1, ga.ap(), axis=AX.X), "ve")
            kb.w(v, "ve")
            kb.inc(v.tensor_tensor(ga.ap(), ga.ap(), m1.unsqueeze(2).broadcast_to(b3), ALU.subtract), "ve")

        @kb.s
        def _(s):
            kb.w(s, "ve")
            kb.inc(s.activation(ga.ap(), ga.ap(), AF.Exp), "se")

        @kb.v
        def _(v):
            kb.w(v, "se")
            kb.inc(v.reduce_sum(m2, ga.ap(), axis=AX.X), "ve")
            kb.w(v, "ve")
            kb.inc(v.reciprocal(m2, m2), "ve")
            kb.w(v, "ve")
            kb.inc(v.tensor_tensor(ga.ap(), ga.ap(), m2.unsqueeze(2).broadcast_to(b3), ALU.mult), "ve")
        emit_dump("ga0sm", ga_t)

        @kb.v
        def _(v):
            kb.w(v, "ve")
            kb.inc(v.reduce_max(m1, ga.ap(), axis=AX.X), "ve")
            kb.w(v, "ve")
            kb.inc(v.tensor_tensor(gsc.ap(), ga.ap(), m1.unsqueeze(2).broadcast_to(b3), ALU.is_ge), "ve")
            kb.w(v, "ve")
            kb.inc(v.reduce_sum(cnt2, gsc.ap(), axis=AX.X), "ve")
            v.scalar_tensor_tensor(gsc.ap(), gsc.ap(), -1e30, ga.ap(), ALU.mult, ALU.add)
            kb.inc(v.reduce_max(m2, gsc.ap(), axis=AX.X), "ve")
            kb.w(v, "ve")
            v.tensor_scalar(cnt2, cnt2, 1.5, None, ALU.is_ge)
            v.tensor_tensor(m1, m1, m2, ALU.subtract)
            v.tensor_tensor(m1, m1, cnt2, ALU.mult)
            kb.inc(v.tensor_tensor(m2, m2, m1, ALU.add), "ve")
            kb.w(v, "ve")
            kb.inc(v.reduce_max(m1, ga.ap(), axis=AX.X), "ve")
            kb.w(v, "ve")
            v.tensor_tensor(gsc.ap(), ga.ap(), m2.unsqueeze(2).broadcast_to(b3), ALU.is_ge)
            v.tensor_tensor(gsc.ap(), gsc.ap(), ga.ap(), ALU.mult)
            kb.inc(v.tensor_tensor(m1, m1, m2, ALU.add), "ve")
            kb.w(v, "ve")
            kb.inc(v.reciprocal(m1, m1), "ve")
            kb.w(v, "ve")
            kb.inc(v.tensor_tensor(cwtm.ap(), gsc.ap(), m1.unsqueeze(2).broadcast_to(b3), ALU.mult), "ve")
        emit_dump("cwtm0", cwtm_t)
        emit_dump("gsel0", gsc_t)

        @kb.t
        def _(t):
            kb.w(t, "ve")
            for tc in range(16):
                i = t.transpose(ps_t[:, 128 * tc:128 * (tc + 1)], cwtm_t[:, tc, :], sb["ident_bf"].ap())
            kb.inc(i, "pe")

        @kb.v
        def _(v):
            kb.w(v, "pe")
            kb.inc(v.tensor_copy(cw0, ps_t.ap()), "ve")
        st.close()

    emit_gate0()
    emit_dump("cw0", cw0)

    def emit_xn2():
        st = ExitStack()
        ps_bc = [st.enter_context(nc.psum_tensor("ps_bc_x2_%d" % i_, [128, 512], F32)) for i_ in range(4)]
        for ti in range(4):
            sl = slice(512 * ti, 512 * (ti + 1))

            @kb.t
            def _(t, ti=ti, sl=sl):
                kb.w(t, "ve")
                t.matmul(ps_bc[2 * (ti % 2)].ap(), sb["c1024_row"].ap(), ssx[0:1, sl], start=True, stop=True)
                kb.inc(t.matmul(ps_bc[2 * (ti % 2) + 1].ap(), sb["onesrow_f"].ap(), r2[0:1, sl],
                                start=True, stop=True), "pe")

            @kb.v
            def _(v, ti=ti, sl=sl):
                kb.w(v, "pe")
                for kc in range(NCH):
                    v.tensor_tensor(sb["lntmp"].ap(), x[:, kc, sl], ps_bc[2 * (ti % 2)].ap(), ALU.subtract)
                    v.tensor_tensor(sb["lntmp"].ap(), sb["lntmp"].ap(), ps_bc[2 * (ti % 2) + 1].ap(), ALU.mult)
                    i = v.tensor_scalar(xn2[:, kc, sl], sb["lntmp"].ap(), sb["gamma_p"][:, kc:kc + 1],
                                        sb["beta_p"][:, kc:kc + 1], ALU.mult, ALU.add)
                kb.inc(i, "ve")
        st.close()

    emit_xn2()
    emit_dump("xn2", xn2)

    @kb.y
    def _(sy):
        kb.w(sy, "ve")
        kb.inc(sy.dma_start(xspill.rearrange("(c p) t -> p c t", p=128), x), "sp", 16)

    def emit_moe0():
        we = [kb.av(0, BF16, [NCH, D]), kb.av(16, BF16, [NCH, D]), kb.av(32, BF16, [NCH, D])]
        rhs = [kb.av(48, BF16, [NCH, 512]), kb.av(56, BF16, [NCH, 512])]
        cwb = kb.av(128, BF16, [E0, 512])
        st = ExitStack()
        ps_m = st.enter_context(nc.psum_tensor("ps_m0", [128, 8, 512], F32))
        pe_chain = {}
        ve_rhs = {}
        se_evac = {}
        wd_we = {}
        ve_cwb = {}

        for ti in range(4):
            sl = slice(512 * ti, 512 * (ti + 1))
            pe_cwb = {}
            for e in range(E0):
                @kb.t
                def _(t, e=e, ti=ti, sl=sl):
                    if ti == 0 and e == 0:
                        kb.w(t, "sp")
                        kb.w(t, "ve")
                    if e >= 2:
                        kb.w(t, "ve", ve_cwb[(ti, e - 2)])
                    if e == 0 and ti > 0:
                        kb.w(t, "se", se_evac[ti - 1])
                    pe_cwb[e] = kb.inc(t.matmul(ps_m[:, e % 2, :], sb["oh8"][:, 128 * e:128 * (e + 1)],
                                                cw0[:, sl], start=True, stop=True), "pe")

                @kb.v
                def _(v, e=e, ti=ti):
                    kb.w(v, "pe", pe_cwb[e])
                    ve_cwb[(ti, e)] = kb.inc(v.tensor_copy(cwb[:, e, :], ps_m[:, e % 2, :]), "ve")

            for e in range(E0):
                k = ti * E0 + e

                @kb.g
                def _(g, ti=ti, e=e, k=k):
                    if ti == 0 and e == 0:
                        kb.w(g, "sp")
                    if k >= 3:
                        kb.w(g, "pe", pe_chain[k - 3])
                    wd_we[k] = kb.inc(g.dma_start(we[k % 3][:, :, :],
                                                  dr["we0"][e].rearrange("(c p) m -> p c m", p=128)), "wd", 16)

                @kb.v
                def _(v, ti=ti, e=e, k=k, sl=sl):
                    kb.w(v, "ve", ve_cwb[(ti, e)])
                    if k >= 2:
                        kb.w(v, "pe", pe_chain[k - 2])
                    for kc in range(NCH):
                        i = v.tensor_tensor(rhs[k % 2][:, kc, :], xn[:, kc, sl], cwb[:, e, :], ALU.mult)
                    ve_rhs[k] = kb.inc(i, "ve")

                @kb.t
                def _(t, ti=ti, e=e, k=k, sl=sl):
                    kb.w(t, "ve", ve_rhs[k])
                    kb.w(t, "wd", wd_we[k])
                    if e == 0:
                        kb.w(t, "ve", ve_cwb[(ti, E0 - 1)])
                        if ti > 0:
                            kb.w(t, "se", se_evac[ti - 1])
                    for oc in range(8):
                        for kc in range(NCH):
                            i = t.matmul(ps_m[:, oc, :], we[k % 3][:, kc, 128 * oc:128 * (oc + 1)],
                                         rhs[k % 2][:, kc, :], start=(e == 0 and kc == 0), stop=False)
                    pe_chain[k] = kb.inc(i, "pe")

            @kb.t
            def _(t, ti=ti, sl=sl):
                for oc in range(8):
                    i = t.matmul(ps_m[:, oc, :], sb["be0_sb"][:, 128 * oc:128 * (oc + 1)], cw0[:, sl],
                                 start=False, stop=True)
                pe_chain[ti * E0 + E0 - 1] = kb.inc(i, "pe")

            @kb.s
            def _(s, ti=ti, sl=sl):
                kb.w(s, "pe", pe_chain[ti * E0 + E0 - 1])
                for oc in range(8):
                    i = s.activation(g0[:, oc, sl], ps_m[:, oc, :], AF.Silu)
                se_evac[ti] = kb.inc(i, "se")
        st.close()

    emit_moe0()
    emit_dump("g0", g0)

    v2 = kb.av(0, BF16, [16, D])
    wqv = kb.av(32, BF16, [NCH, D])
    wkv_ = kb.av(48, BF16, [NCH, D])
    wv2 = kb.av(64, BF16, [NCH, D])
    qh = kb.av(80, BF16, [T])
    khb = kb.av(84, BF16, [T])
    pT = kb.av(88, BF16, [2, 1024])
    att2 = kb.av(92, BF16, [1024])
    rd = kb.av(64, F32, [1024])
    bv2 = kb.row(132, BF16, D)
    bvt = kb.row(174, BF16, D * H)
    wvt2 = [kb.av(174, BF16, [8192]), kb.av(0, BF16, [8192])]

    def emit_attn0():
        st = ExitStack()
        ve_red = {}

        @kb.g
        def _(g):
            kb.w(g, "se"); kb.w(g, "pe"); kb.w(g, "ve")
            kb.inc(g.dma_start(wqv[:, :, :], dr["wq"].rearrange("(c p) m -> p c m", p=128)), "wd", 16)
            kb.inc(g.dma_start(wkv_[:, :, :], dr["wk0"].rearrange("(c p) m -> p c m", p=128)), "wd", 16)
            kb.inc(g.dma_start(bvt, dr["bv_row"]), "wd", 16)

        @kb.v
        def _(v):
            kb.w(v, "wd")
            with nc.allow_low_precision(reason="bf16 grouped-sum, fp32 internal accum"):
                kb.inc(v.reduce_sum(bv2, bvt.rearrange("a (m u) -> a m u", u=8), axis=AX.X), "ve")

        ve_bv2 = kb.cnt["ve"]
        wd_wv = {}
        for kc in range(NCH):
            @kb.g
            def _(g, kc=kc):
                if kc == 0:
                    kb.w(g, "ve", ve_bv2)
                if kc >= 2:
                    kb.w(g, "ve", ve_red[kc - 2])
                wd_wv[kc] = kb.inc(g.dma_start(wvt2[kc % 2][:, :], dr["wv0"][128 * kc:128 * (kc + 1), :]), "wd", 16)

            @kb.v
            def _(v, kc=kc):
                kb.w(v, "wd", wd_wv[kc])
                with nc.allow_low_precision(reason="bf16 grouped-sum, fp32 internal accum"):
                    ve_red[kc] = kb.inc(v.reduce_sum(wv2[:, kc, :], wvt2[kc % 2].rearrange("p (m u) -> p m u", u=8),
                                                     axis=AX.X), "ve")

        st_v = ExitStack()
        ps_v2 = [st_v.enter_context(nc.psum_tensor("ps_v2_%d" % i_, [128, 512], F32)) for i_ in range(2)]
        ve_v2 = {}
        for tc in range(16):
            for hf in range(2):
                j = tc * 2 + hf

                @kb.t
                def _(t, tc=tc, hf=hf, j=j):
                    if j == 0:
                        kb.w(t, "ve")
                    if j >= 2:
                        kb.w(t, "ve", ve_v2[j - 2])
                    for kc in range(NCH):
                        t.matmul(ps_v2[j % 2].ap(), xn2[:, kc, 128 * tc:128 * (tc + 1)],
                                 wv2[:, kc, 512 * hf:512 * (hf + 1)], start=(kc == 0), stop=False)
                    kb.inc(t.matmul(ps_v2[j % 2].ap(), sb["ones_bf"][0:1, 0:128],
                                    bv2[0:1, 512 * hf:512 * (hf + 1)], start=False, stop=True), "pe")

                @kb.v
                def _(v, tc=tc, hf=hf, j=j):
                    kb.w(v, "pe")
                    ve_v2[j] = kb.inc(v.tensor_copy(v2[:, tc, 512 * hf:512 * (hf + 1)], ps_v2[j % 2].ap()), "ve")
        st_v.close()

        st_h = ExitStack()
        ps_sc = [st_h.enter_context(nc.psum_tensor("ps_sc_%d" % i_, [128, 1024], F32)) for i_ in range(2)]
        ps_att = st_h.enter_context(nc.psum_tensor("ps_att", [128, 1024], F32))
        ps_den = st_h.enter_context(nc.psum_tensor("ps_den", [128, 1024], F32))

        st_attn = {}
        for h in range(H):
            ve_evac = {}
            for wi, (wmat, bias, dst) in enumerate(((wqv, "bq_p", qh), (wkv_, "bk_p", khb))):
                for ti in range(4):
                    pe_p = {}
                    jch = wi * 4 + ti

                    @kb.t
                    def _(t, wmat=wmat, h=h, ti=ti, wi=wi, jch=jch, pe_p=pe_p, st_=st_attn):
                        if jch < 2:
                            kb.w(t, "ve", st_.get("ve_att2") or kb.cnt["ve"])
                        else:
                            kb.w(t, "ve", ve_evac[jch - 2])
                        ps = ps_att[:, 0:512] if ti % 2 == 0 else ps_den[:, 0:512]
                        for kc in range(NCH):
                            i = t.matmul(ps, wmat[:, kc, 128 * h:128 * (h + 1)],
                                         xn2[:, kc, 512 * ti:512 * (ti + 1)], start=(kc == 0), stop=(kc == NCH - 1))
                        pe_p[0] = kb.inc(i, "pe")

                    @kb.v
                    def _(v, bias=bias, dst=dst, h=h, ti=ti, wi=wi, jch=jch, pe_p=pe_p):
                        kb.w(v, "pe", pe_p[0])
                        ps = ps_att[:, 0:512] if ti % 2 == 0 else ps_den[:, 0:512]
                        ve_evac[jch] = kb.inc(v.tensor_scalar(dst[:, 512 * ti:512 * (ti + 1)], ps,
                                               sb[bias][:, h:h + 1], None, ALU.add), "ve")

            for sh in range(2):
                pe_sc = {}
                se_exp = {}
                pe_pv = {}

                def emit_scores(tc, h, sh):
                    @kb.t
                    def _(t, h=h, sh=sh, tc=tc, st_=st_attn):
                        if tc == 0:
                            kb.w(t, "ve", ve_evac[7] if sh == 0 else st_["ve_att2"])
                        if tc >= 2:
                            kb.w(t, "se", se_exp[tc - 2])
                        for half in range(2):
                            i = t.matmul(ps_sc[tc % 2][:, 512 * half:512 * (half + 1)],
                                         khb[:, 128 * tc:128 * (tc + 1)],
                                         qh[:, 1024 * sh + 512 * half:1024 * sh + 512 * (half + 1)],
                                         start=True, stop=True)
                        pe_sc[tc] = kb.inc(i, "pe")

                def emit_exp(tc):
                    @kb.s
                    def _(s, tc=tc):
                        kb.w(s, "pe", pe_sc[tc])
                        if tc >= 2:
                            kb.w(s, "pe", pe_pv[tc - 2])
                        se_exp[tc] = kb.inc(s.activation(pT[:, tc % 2, :], ps_sc[tc % 2].ap(), AF.Exp,
                                                         bias=sb["negm_col"].ap(), scale=SQDK), "se")

                def emit_pv(tc, h):
                    @kb.t
                    def _(t, h=h, tc=tc):
                        kb.w(t, "se", se_exp[tc])
                        for half in range(2):
                            t.matmul(ps_att[:, 512 * half:512 * (half + 1)], v2[:, tc, 128 * h:128 * (h + 1)],
                                     pT[:, tc % 2, 512 * half:512 * (half + 1)], start=(tc == 0), stop=(tc == 15))
                        for half in range(2):
                            i = t.matmul(ps_den[:, 512 * half:512 * (half + 1)], sb["ones_bf"].ap(),
                                         pT[:, tc % 2, 512 * half:512 * (half + 1)], start=(tc == 0), stop=(tc == 15))
                        pe_pv[tc] = kb.inc(i, "pe")

                for tc in range(16):
                    emit_scores(tc, h, sh)
                    emit_exp(tc)
                    if tc >= 1:
                        emit_pv(tc - 1, h)
                emit_pv(15, h)

                @kb.v
                def _(v, h=h, sh=sh, st_=st_attn):
                    kb.w(v, "pe")
                    v.reciprocal(rd, ps_den.ap())
                    kb.inc(v.tensor_tensor(att2, ps_att.ap(), rd, ALU.mult), "ve")
                    st_["ve_att2"] = kb.cnt["ve"]

                @kb.v
                def _(v, h=h, sh=sh):
                    j0 = 256 * h + 128 * sh
                    for k1 in range(NCH):
                        i = v.tensor_tensor(sgv(k1)[:, j0:j0 + 128], att2[:, k1::8],
                                            g0[:, k1, j0:j0 + 128], ALU.mult)
                    kb.inc(i, "ve")
        st_h.close()
        st.close()

    emit_attn0()

    @kb.y
    def _(sy):
        kb.w(sy, "ve"); kb.w(sy, "pe"); kb.w(sy, "se")
        kb.inc(sy.dma_start(x, xspill.rearrange("(c p) t -> p c t", p=128)), "sp", 16)

    @kb.v
    def _(v):
        kb.w(v, "sp")
        for kc in range(NCH):
            for ti in range(4):
                sl = slice(512 * ti, 512 * (ti + 1))
                i = v.tensor_tensor(x[:, kc, sl], x[:, kc, sl], sgv(kc)[:, sl], ALU.add)
        kb.inc(i, "ve")

    emit_dump("x1", x)

    emit_ln("l1", want_r2=False)
    emit_dump("xn1", xn)

    def emit_gate1():
        st = ExitStack()
        ga_t = kb.av(143, F32, [16, E1])
        gb_t = kb.av(144, F32, [16, E1])
        gsc_t = kb.av(145, F32, [16, E1])
        cwtm_t = kb.av(146, BF16, [16, E1])
        ga = type("W", (), {"ap": staticmethod(lambda: ga_t)})()
        gb = type("W", (), {"ap": staticmethod(lambda: gb_t)})()
        gsc = type("W", (), {"ap": staticmethod(lambda: gsc_t)})()
        cwtm = type("W", (), {"ap": staticmethod(lambda: cwtm_t)})()
        ps_g = st.enter_context(nc.psum_tensor("ps_g1", [128, 16, E1], F32))
        ps_b = st.enter_context(nc.psum_tensor("ps_b1", [128, 16], F32))
        ps_t = st.enter_context(nc.psum_tensor("ps_t1", [E1, T], BF16))
        m1 = sb["m1"].ap()
        m2 = sb["m2"].ap()
        cnt2 = sb["cnt2"].ap()
        gm1, gm2, gcnt, gs = sb["gm1"].ap(), sb["gm2"].ap(), sb["gcnt"].ap(), sb["gs1"].ap()
        b3 = [128, 16, E1]
        b4 = [128, 16, 4, 4]
        b34 = [128, 16, 4]
        gb4 = gb_t.rearrange("p a b -> p a b") if False else gb.ap().rearrange("p t (g u) -> p t g u", u=4)
        g4 = gsc.ap().rearrange("p t (g u) -> p t g u", u=4)

        emit_scores_f32(E1, "gwT", ps_g, ga_t, st)

        @kb.t
        def _(t):
            kb.w(t, "ve")
            kb.inc(t.matmul(ps_b.ap()[:, 0:E1] if E1 <= 16 else ps_b.ap(), sb["onesrow_f"].ap(),
                            sb["gb_row"].ap(), start=True, stop=True), "pe")

        @kb.v
        def _(v):
            kb.w(v, "pe")
            kb.inc(v.tensor_copy(sb["gbb"].ap(), ps_b.ap()[:, 0:E1]), "ve")
            kb.w(v, "ve")
            kb.inc(v.reduce_max(m1, ga.ap(), axis=AX.X), "ve")
            kb.w(v, "ve")
            kb.inc(v.tensor_tensor(ga.ap(), ga.ap(), m1.unsqueeze(2).broadcast_to(b3), ALU.subtract), "ve")

        @kb.s
        def _(s):
            kb.w(s, "ve")
            kb.inc(s.activation(ga.ap(), ga.ap(), AF.Exp), "se")

        @kb.v
        def _(v):
            kb.w(v, "se")
            kb.inc(v.reduce_sum(m2, ga.ap(), axis=AX.X), "ve")
            kb.w(v, "ve")
            kb.inc(v.reciprocal(m2, m2), "ve")
            kb.w(v, "ve")
            kb.inc(v.tensor_tensor(ga.ap(), ga.ap(), m2.unsqueeze(2).broadcast_to(b3), ALU.mult), "ve")
            kb.w(v, "ve")
            kb.inc(v.tensor_tensor(gb.ap(), ga.ap(), sb["gbb"].ap().unsqueeze(1).broadcast_to(b3), ALU.add), "ve")
            kb.w(v, "ve")
            kb.inc(v.reduce_max(gm1, gb4, axis=AX.X), "ve")
            kb.w(v, "ve")
            kb.inc(v.tensor_tensor(g4, gb4, gm1.unsqueeze(3).broadcast_to(b4), ALU.is_ge), "ve")
            kb.w(v, "ve")
            kb.inc(v.reduce_sum(gcnt, g4, axis=AX.X), "ve")
            v.scalar_tensor_tensor(gsc.ap(), gsc.ap(), -1e30, gb.ap(), ALU.mult, ALU.add)
            kb.inc(v.reduce_max(gm2, g4, axis=AX.X), "ve")
            kb.w(v, "ve")
            v.tensor_scalar(gcnt, gcnt, 1.5, None, ALU.is_ge)
            v.tensor_tensor(gm1, gm1, gm2, ALU.subtract)
            v.tensor_tensor(gm1, gm1, gcnt, ALU.mult)
            kb.inc(v.tensor_tensor(gm2, gm2, gm1, ALU.add), "ve")
            kb.w(v, "ve")
            kb.inc(v.reduce_max(gm1, gb4, axis=AX.X), "ve")
            kb.w(v, "ve")
            kb.inc(v.tensor_tensor(gs, gm1, gm2, ALU.add), "ve")
            kb.w(v, "ve")
            kb.inc(v.reduce_max(m1, gs, axis=AX.X), "ve")
            kb.w(v, "ve")
            kb.inc(v.tensor_tensor(gm1, gs, m1.unsqueeze(2).broadcast_to(b34), ALU.is_ge), "ve")
            kb.w(v, "ve")
            kb.inc(v.reduce_sum(cnt2, gm1, axis=AX.X), "ve")
            v.scalar_tensor_tensor(gm1, gm1, -1e30, gs, ALU.mult, ALU.add)
            kb.inc(v.reduce_max(m2, gm1, axis=AX.X), "ve")
            kb.w(v, "ve")
            v.tensor_scalar(cnt2, cnt2, 1.5, None, ALU.is_ge)
            v.tensor_tensor(m1, m1, m2, ALU.subtract)
            v.tensor_tensor(m1, m1, cnt2, ALU.mult)
            kb.inc(v.tensor_tensor(m2, m2, m1, ALU.add), "ve")
            kb.w(v, "ve")
            kb.inc(v.tensor_tensor(gm1, gs, m2.unsqueeze(2).broadcast_to(b34), ALU.is_ge), "ve")
            kb.w(v, "ve")
            kb.inc(v.tensor_scalar(gm1, gm1, 1e30, -1e30, ALU.mult, ALU.add), "ve")
            kb.w(v, "ve")
            kb.inc(v.tensor_tensor(g4, gm1.unsqueeze(3).broadcast_to(b4), gb4, ALU.add), "ve")
            kb.w(v, "ve")
            kb.inc(v.reduce_max(m1, gsc.ap(), axis=AX.X), "ve")
            kb.w(v, "ve")
            kb.inc(v.tensor_tensor(gb.ap(), gsc.ap(), m1.unsqueeze(2).broadcast_to(b3), ALU.is_ge), "ve")
            kb.w(v, "ve")
            kb.inc(v.reduce_sum(cnt2, gb.ap(), axis=AX.X), "ve")
            v.scalar_tensor_tensor(gb.ap(), gb.ap(), -1e30, gsc.ap(), ALU.mult, ALU.add)
            kb.inc(v.reduce_max(m2, gb.ap(), axis=AX.X), "ve")
            kb.w(v, "ve")
            v.tensor_scalar(cnt2, cnt2, 1.5, None, ALU.is_ge)
            v.tensor_tensor(m1, m1, m2, ALU.subtract)
            v.tensor_tensor(m1, m1, cnt2, ALU.mult)
            kb.inc(v.tensor_tensor(m2, m2, m1, ALU.add), "ve")
            kb.w(v, "ve")
            v.tensor_tensor(gb.ap(), gsc.ap(), m2.unsqueeze(2).broadcast_to(b3), ALU.is_ge)
            kb.inc(v.tensor_tensor(cwtm.ap(), gb.ap(), ga.ap(), ALU.mult), "ve")

        @kb.t
        def _(t):
            kb.w(t, "ve")
            for tc in range(16):
                i = t.transpose(ps_t[:, 128 * tc:128 * (tc + 1)], cwtm_t[:, tc, :], sb["ident_bf"].ap())
            kb.inc(i, "pe")

        @kb.v
        def _(v):
            kb.w(v, "pe")
            kb.inc(v.tensor_copy(cw1, ps_t.ap()), "ve")
        st.close()

    emit_gate1()
    emit_dump("cw1", cw1)

    @kb.y
    def _(sy):
        kb.w(sy, "ve")
        kb.inc(sy.dma_start(xspill.rearrange("(c p) t -> p c t", p=128), x), "sp", 16)

    wkT = kb.av(0, BF16, [NCH, D])
    wvT = kb.av(16, BF16, [NCH, D])
    wrT = kb.av(32, BF16, [NCH, D])
    woT = kb.av(48, BF16, [NCH, D])
    mixa = kb.av(96, BF16, [NCH, SEG])
    mixb = kb.av(100, BF16, [NCH, SEG])
    mixc = kb.av(120, BF16, [NCH, SEG])
    rwk = kb.av(124, BF16, [NCH, SEG])
    kbuf = kb.av(104, F32, [NCH, SEG])
    vbuf = kb.av(112, F32, [NCH, SEG])
    srb = kb.av(128, F32, [NCH, SEG])
    Ab = kb.av(142, F32, [NCH, SEG])
    Bb = kb.av(96, F32, [NCH, SEG])
    numb = kb.av(150, F32, [NCH, SEG])
    satt = kb.av(158, BF16, [NCH, T])

    def emit_wkv():
        st = ExitStack()
        ps_p = [st.enter_context(nc.psum_tensor("ps_p8_%d" % i_, [128, SEG], F32)) for i_ in range(2)]
        ps_o = [st.enter_context(nc.psum_tensor("ps_o8_%d" % i_, [128, SEG], F32)) for i_ in range(4)]

        @kb.g
        def _(g):
            kb.w(g, "sp")
            kb.w(g, "ve")
            kb.inc(g.dma_start(wkT[:, :, :], dr["wkT"].rearrange("(c p) m -> p c m", p=128)), "wd", 16)
            kb.inc(g.dma_start(wvT[:, :, :], dr["wvT"].rearrange("(c p) m -> p c m", p=128)), "wd", 16)
            kb.inc(g.dma_start(wrT[:, :, :], dr["wrT"].rearrange("(c p) m -> p c m", p=128)), "wd", 16)
            kb.inc(g.dma_start(woT[:, :, :], dr["woT"].rearrange("(c p) m -> p c m", p=128)), "wd", 16)

        sp_x8 = kb.cnt["sp"]

        def mix_d(v, sgi, t0):
            for kc in range(NCH):
                if sgi == 0:
                    v.tensor_copy(numb[:, kc, 0:1], xn[:, kc, 0:1])
                    i = v.tensor_tensor(numb[:, kc, 1:SEG], xn[:, kc, 1:SEG], xn[:, kc, 0:SEG - 1], ALU.subtract)
                else:
                    i = v.tensor_tensor(numb[:, kc, :], xn[:, kc, t0:t0 + SEG],
                                        xn[:, kc, t0 - 1:t0 + SEG - 1], ALU.subtract)
            return i

        def mixes(v, dst, tmn, tm1n, sgi, t0):
            for kc in range(NCH):
                if sgi == 0:
                    v.tensor_scalar(dst[:, kc, 0:1], xn[:, kc, 0:1], sb[tmn][:, kc:kc + 1], None, ALU.mult)
                    i = v.scalar_tensor_tensor(dst[:, kc, 1:SEG], numb[:, kc, 1:SEG],
                                               sb[tmn][:, kc:kc + 1], xn[:, kc, 0:SEG - 1], ALU.mult, ALU.add)
                else:
                    i = v.scalar_tensor_tensor(dst[:, kc, :], numb[:, kc, :],
                                               sb[tmn][:, kc:kc + 1], xn[:, kc, t0 - 1:t0 + SEG - 1], ALU.mult, ALU.add)
            return i

        state = {"pst": None}

        def emit_wo():
            pst = state["pst"]
            if pst is None:
                return
            t0p = pst["t0"]
            pe_o = {}
            ve_o = {}
            for oc in range(NCH):
                @kb.t
                def _(t, oc=oc):
                    if oc == 0:
                        kb.w(t, "ve", pst["ve_rwk"])
                    if oc >= 4:
                        kb.w(t, "ve", ve_o[oc - 4])
                    for kc in range(NCH):
                        i = t.matmul(ps_o[oc % 4].ap(), woT[:, kc, 128 * oc:128 * (oc + 1)],
                                     rwk[:, kc, :], start=(kc == 0), stop=(kc == NCH - 1))
                    pe_o[oc] = kb.inc(i, "pe")

                @kb.v
                def _(v, oc=oc):
                    kb.w(v, "pe", pe_o[oc])
                    ve_o[oc] = kb.inc(v.tensor_copy(satt[:, oc, t0p:t0p + SEG], ps_o[oc % 4].ap()), "ve")
            pst["ve_o_last"] = ve_o[NCH - 1]
            pst["pe_o_last"] = pe_o[NCH - 1]

        for sgi in range(NSEG):
            t0 = SEG * sgi
            pst = state["pst"]
            cur = {"t0": t0}

            @kb.v
            def _(v, sgi=sgi, t0=t0, pst=pst):
                if sgi == 0:
                    kb.w(v, "sp", sp_x8)
                    kb.w(v, "pe")
                else:
                    kb.w(v, "pe", pst["pe_r_last"])
                mix_d(v, sgi, t0)
                mixes(v, mixa, "tmk_p", "tmk1", sgi, t0)
                mixes(v, mixb, "tmv_p", "tmv1", sgi, t0)
                kb.inc(mixes(v, mixc, "tmr_p", "tmr1", sgi, t0), "ve")

            ve_mix = kb.cnt["ve"]
            emit_wo()

            pe_pr = {}
            ve_ev = {}
            for pi, (wnm, srcb, dstb) in enumerate(((wkT, mixa, kbuf), (wvT, mixb, vbuf))):
                for oc in range(NCH):
                    j = pi * NCH + oc

                    @kb.t
                    def _(t, wnm=wnm, srcb=srcb, oc=oc, j=j, sgi=sgi, pst=pst):
                        if j == 0:
                            kb.w(t, "ve", ve_mix)
                            if sgi == 0:
                                kb.w(t, "wd")
                            else:
                                kb.w(t, "se", pst["se_ek"])
                                kb.w(t, "ve", pst["ve_scan"])
                        if j >= 2:
                            kb.w(t, "se", ve_ev[j - 2])
                        elif sgi > 0:
                            kb.w(t, "se", pst["se_r_last"])
                        for kc in range(NCH):
                            i = t.matmul(ps_p[j % 2].ap(), wnm[:, kc, 128 * oc:128 * (oc + 1)],
                                         srcb[:, kc, :], start=(kc == 0), stop=(kc == NCH - 1))
                        pe_pr[j] = kb.inc(i, "pe")

                    @kb.s
                    def _(s, dstb=dstb, oc=oc, j=j):
                        kb.w(s, "pe", pe_pr[j])
                        ve_ev[j] = kb.inc(s.activation(dstb[:, oc, :], ps_p[j % 2].ap(), AF.Copy), "se")

            pe_r = {}
            se_r = {}
            for oc in range(NCH):
                @kb.t
                def _(t, oc=oc):
                    if oc < 2:
                        kb.w(t, "se", ve_ev[2 * NCH - 1])
                    if oc >= 2:
                        kb.w(t, "se", se_r[oc - 2])
                    for kc in range(NCH):
                        i = t.matmul(ps_p[oc % 2].ap(), wrT[:, kc, 128 * oc:128 * (oc + 1)],
                                     mixc[:, kc, :], start=(kc == 0), stop=(kc == NCH - 1))
                    pe_r[oc] = kb.inc(i, "pe")

                @kb.s
                def _(s, oc=oc):
                    kb.w(s, "pe", pe_r[oc])
                    se_r[oc] = kb.inc(s.activation(srb[:, oc, :], ps_p[oc % 2].ap(), AF.Sigmoid), "se")
            cur["pe_r_last"] = pe_r[NCH - 1]
            cur["se_r_last"] = se_r[NCH - 1]

            @kb.s
            def _(s):
                cur["se_ek"] = kb.inc(s.activation(kbuf[:, :, :], kbuf[:, :, :], AF.Exp), "se")

            @kb.v
            def _(v, sgi=sgi, cur=cur):
                kb.w(v, "se", cur["se_ek"])
                kb.inc(v.tensor_tensor(vbuf[:, :, :], kbuf[:, :, :], vbuf[:, :, :], ALU.mult), "ve")
                for kc in range(NCH):
                    ini = 0.0 if sgi == 0 else sb["carA"][:, kc:kc + 1]
                    v.tensor_tensor_scan(Ab[:, kc, :], sb["ew"][:, kc:kc + 1].broadcast_to([128, SEG]),
                                         vbuf[:, kc, :], ini, ALU.mult, ALU.add)
                for kc in range(NCH):
                    ini = 0.0 if sgi == 0 else sb["carB"][:, kc:kc + 1]
                    i = v.tensor_tensor_scan(Bb[:, kc, :], sb["ew"][:, kc:kc + 1].broadcast_to([128, SEG]),
                                             kbuf[:, kc, :], ini, ALU.mult, ALU.add)
                kb.inc(i, "ve")
                for kc in range(NCH):
                    a0 = sb["zero_col"].ap() if sgi == 0 else sb["carA"][:, kc:kc + 1]
                    b0 = sb["zero_col"].ap() if sgi == 0 else sb["carB"][:, kc:kc + 1]
                    v.scalar_tensor_tensor(numb[:, kc, 0:1], vbuf[:, kc, 0:1], sb["eu"][:, kc:kc + 1],
                                           a0, ALU.mult, ALU.add)
                    v.scalar_tensor_tensor(numb[:, kc, 1:SEG], vbuf[:, kc, 1:SEG], sb["eu"][:, kc:kc + 1],
                                           Ab[:, kc, 0:SEG - 1], ALU.mult, ALU.add)
                    v.scalar_tensor_tensor(vbuf[:, kc, 0:1], kbuf[:, kc, 0:1], sb["eu"][:, kc:kc + 1],
                                           b0, ALU.mult, ALU.add)
                    i = v.scalar_tensor_tensor(vbuf[:, kc, 1:SEG], kbuf[:, kc, 1:SEG], sb["eu"][:, kc:kc + 1],
                                               Bb[:, kc, 0:SEG - 1], ALU.mult, ALU.add)
                v.tensor_copy(sb["carA"].ap(), Ab[:, :, SEG - 1])
                cur["ve_scan"] = kb.inc(v.tensor_copy(sb["carB"].ap(), Bb[:, :, SEG - 1]), "ve")

            @kb.s
            def _(s, cur=cur):
                kb.w(s, "ve", cur["ve_scan"])
                s.activation(vbuf[:, :, :], vbuf[:, :, :], AF.Ln)
                kb.inc(s.activation(vbuf[:, :, :], vbuf[:, :, :], AF.Exp, scale=-1.0), "se")

            @kb.v
            def _(v, sgi=sgi, cur=cur, pst=pst):
                kb.w(v, "se")
                if sgi > 0:
                    kb.w(v, "ve", pst.get("ve_o_last", 0))
                v.tensor_tensor(numb[:, :, :], numb[:, :, :], vbuf[:, :, :], ALU.mult)
                cur["ve_rwk"] = kb.inc(v.tensor_tensor(rwk[:, :, :], numb[:, :, :], srb[:, :, :], ALU.mult), "ve")

            state["pst"] = cur

        emit_wo()
        st.close()

    emit_wkv()
    emit_dump("satt", satt)

    y_acc = kb.av(0, F32, [NCH, 1024])
    we1b = kb.av(32, BF16, [NCH, I1])
    we3b = kb.av(40, BF16, [NCH, I1])
    we2b = [kb.av(48, BF16, [4, D]), kb.av(56, BF16, [4, D])]
    h1b = kb.av(96, BF16, [4, 1024])
    hsc = [kb.av(104, BF16, [4, 1024]), kb.av(112, BF16, [4, 1024])]
    cwbt = kb.av(120, F32, [2, 512])
    wsb = kb.av(104, BF16, [NCH, D])
    shh = kb.av(48, BF16, [NCH, 1024])

    def emit_ffn1():
        st = ExitStack()
        ps_h = [st.enter_context(nc.psum_tensor("ps_h_%d" % i_, [128, 512], F32)) for i_ in range(2)]
        ps_y = [st.enter_context(nc.psum_tensor("ps_y_%d" % i_, [128, 512], F32)) for i_ in range(2)]
        ps_cw = st.enter_context(nc.psum_tensor("ps_cw9", [128, 512], F32))
        NE = 2
        for hf in range(2):
            th0 = 1024 * hf

            wd_s = {}

            @kb.g
            def _(g, hf=hf):
                kb.w(g, "pe")
                kb.w(g, "ve")
                wd_s[1] = kb.inc(g.dma_start(wsb[:, :, :], dr["ws1"].rearrange("(c p) m -> p c m", p=128)), "wd", 16)

            pe_s = {}
            se_s = {}
            for ic in range(NCH):
                for tix in range(2):
                    j = ic * 2 + tix
                    slg = slice(th0 + 512 * tix, th0 + 512 * (tix + 1))

                    @kb.t
                    def _(t, ic=ic, tix=tix, j=j, slg=slg):
                        if j == 0:
                            kb.w(t, "wd", wd_s[1])
                            kb.w(t, "se")
                            kb.w(t, "ve")
                        if j >= 2:
                            kb.w(t, "se", se_s[j - 2])
                        for kc in range(NCH):
                            i = t.matmul(ps_h[j % 2].ap(), wsb[:, kc, 128 * ic:128 * (ic + 1)],
                                         xn[:, kc, slg], start=(kc == 0), stop=(kc == NCH - 1))
                        pe_s[j] = kb.inc(i, "pe")

                    @kb.s
                    def _(s, ic=ic, tix=tix, j=j):
                        kb.w(s, "pe", pe_s[j])
                        se_s[j] = kb.inc(s.activation(shh[:, ic, 512 * tix:512 * (tix + 1)], ps_h[j % 2].ap(),
                                                      AF.Silu, bias=sb["bs1_p"][:, ic:ic + 1]), "se")

            @kb.g
            def _(g):
                kb.w(g, "pe")
                wd_s[3] = kb.inc(g.dma_start(wsb[:, :, :], dr["ws3"].rearrange("(c p) m -> p c m", p=128)), "wd", 16)

            pe_s3 = {}
            ve_s3 = {}
            for ic in range(NCH):
                for tix in range(2):
                    j = ic * 2 + tix
                    slg = slice(th0 + 512 * tix, th0 + 512 * (tix + 1))

                    @kb.t
                    def _(t, ic=ic, tix=tix, j=j, slg=slg):
                        if j == 0:
                            kb.w(t, "wd", wd_s[3])
                            kb.w(t, "se", se_s[2 * NCH - 1])
                        if j >= 2:
                            kb.w(t, "ve", ve_s3[j - 2])
                        for kc in range(NCH):
                            i = t.matmul(ps_h[j % 2].ap(), wsb[:, kc, 128 * ic:128 * (ic + 1)],
                                         xn[:, kc, slg], start=(kc == 0), stop=(kc == NCH - 1))
                        pe_s3[j] = kb.inc(i, "pe")

                    @kb.v
                    def _(v, ic=ic, tix=tix, j=j):
                        kb.w(v, "pe", pe_s3[j])
                        v.tensor_scalar(sb["htmp"].ap(), ps_h[j % 2].ap(), sb["bs3_p"][:, ic:ic + 1], None, ALU.add)
                        ve_s3[j] = kb.inc(v.tensor_tensor(shh[:, ic, 512 * tix:512 * (tix + 1)],
                                                          shh[:, ic, 512 * tix:512 * (tix + 1)],
                                                          sb["htmp"].ap(), ALU.mult), "ve")

            @kb.g
            def _(g):
                kb.w(g, "pe")
                wd_s[2] = kb.inc(g.dma_start(wsb[:, :, :], dr["ws2"].rearrange("(c p) m -> p c m", p=128)), "wd", 16)

            pe_s2 = {}
            ve_s2 = {}
            for dc in range(NCH):
                for tix in range(2):
                    j = dc * 2 + tix
                    sl = slice(512 * tix, 512 * (tix + 1))

                    @kb.t
                    def _(t, dc=dc, tix=tix, j=j):
                        if j == 0:
                            kb.w(t, "wd", wd_s[2])
                            kb.w(t, "ve")
                        if j >= 2:
                            kb.w(t, "ve", ve_s2[j - 2])
                        for kc in range(NCH):
                            i = t.matmul(ps_y[j % 2].ap(), wsb[:, kc, 128 * dc:128 * (dc + 1)],
                                         shh[:, kc, sl], start=(kc == 0), stop=(kc == NCH - 1))
                        pe_s2[j] = kb.inc(i, "pe")

                    @kb.v
                    def _(v, dc=dc, j=j, sl=sl):
                        kb.w(v, "pe", pe_s2[j])
                        ve_s2[j] = kb.inc(v.tensor_scalar(y_acc[:, dc, sl], ps_y[j % 2].ap(),
                                                          sb["bs2_p"][:, dc:dc + 1], None, ALU.add), "ve")

            wq9 = {"w1": {}, "w3": {}, "w2": {}}

            def load_w1(e, wait_pe):
                if e >= E1:
                    return
                @kb.g
                def _(g, e=e, wait_pe=wait_pe):
                    kb.w(g, "pe", wait_pe)
                    wq9["w1"][e] = kb.inc(g.dma_start(we1b[:, :, :], dr["we1"][e].rearrange("(c p) m -> p c m", p=128)), "wd", 16)

            def load_w3(e, wait_pe):
                if e >= E1:
                    return
                @kb.g
                def _(g, e=e, wait_pe=wait_pe):
                    kb.w(g, "pe", wait_pe)
                    wq9["w3"][e] = kb.inc(g.dma_start(we3b[:, :, :], dr["we3"][e].rearrange("(c p) m -> p c m", p=128)), "wd", 16)

            def load_w2(e, ei, wait_pe):
                if e >= E1:
                    return
                @kb.g
                def _(g, e=e, ei=ei, wait_pe=wait_pe):
                    kb.w(g, "pe", wait_pe)
                    wq9["w2"][e] = kb.inc(g.dma_start(we2b[ei][:, :, :], dr["we2"][e].rearrange("(c p) m -> p c m", p=128)), "wd", 16)

            load_w1(0, kb.cnt["pe"])
            load_w3(0, kb.cnt["pe"])
            load_w2(0, 0, kb.cnt["pe"])
            load_w2(1, 1, kb.cnt["pe"])

            for grp in range(E1 // NE):
                ve_hsc = {}
                for ei in range(NE):
                    e = grp * NE + ei

                    pe_cwb = {}
                    for tix in range(2):
                        slg = slice(th0 + 512 * tix, th0 + 512 * (tix + 1))

                        @kb.t
                        def _(t, e=e, tix=tix, slg=slg):
                            kb.w(t, "ve")
                            pe_cwb[tix] = kb.inc(t.matmul(ps_cw.ap(), sb["oh16"][:, 128 * e:128 * (e + 1)],
                                                          cw1[:, slg], start=True, stop=True), "pe")

                        @kb.v
                        def _(v, tix=tix):
                            kb.w(v, "pe", pe_cwb[tix])
                            kb.inc(v.tensor_copy(cwbt[:, tix, :], ps_cw.ap()), "ve")

                    ve_cwb = kb.cnt["ve"]
                    pe_h1 = {}
                    se_h1 = {}
                    for ic in range(4):
                        for tix in range(2):
                            j = ic * 2 + tix
                            slg = slice(th0 + 512 * tix, th0 + 512 * (tix + 1))

                            @kb.t
                            def _(t, e=e, ic=ic, tix=tix, j=j, slg=slg):
                                if j == 0:
                                    kb.w(t, "wd", wq9["w1"][e])
                                    kb.w(t, "se")
                                    kb.w(t, "ve")
                                if j >= 2:
                                    kb.w(t, "se", se_h1[j - 2])
                                for kc in range(NCH):
                                    i = t.matmul(ps_h[j % 2].ap(), we1b[:, kc, 128 * ic:128 * (ic + 1)],
                                                 xn[:, kc, slg], start=(kc == 0), stop=(kc == NCH - 1))
                                pe_h1[j] = kb.inc(i, "pe")

                            @kb.s
                            def _(s, e=e, ic=ic, tix=tix, j=j):
                                kb.w(s, "pe", pe_h1[j])
                                se_h1[j] = kb.inc(s.activation(h1b[:, ic, 512 * tix:512 * (tix + 1)],
                                                               ps_h[j % 2].ap(), AF.Silu,
                                                               bias=sb["be1_p"][:, 4 * e + ic:4 * e + ic + 1]), "se")

                    load_w1(e + 1, pe_h1[7])
                    pe_h3 = {}
                    ve_h3 = {}
                    for ic in range(4):
                        for tix in range(2):
                            j = ic * 2 + tix
                            slg = slice(th0 + 512 * tix, th0 + 512 * (tix + 1))

                            @kb.t
                            def _(t, e=e, ic=ic, tix=tix, j=j, slg=slg):
                                if j == 0:
                                    kb.w(t, "wd", wq9["w3"][e])
                                    kb.w(t, "se", se_h1[7])
                                if j >= 2:
                                    kb.w(t, "ve", ve_h3[j - 2])
                                for kc in range(NCH):
                                    i = t.matmul(ps_h[j % 2].ap(), we3b[:, kc, 128 * ic:128 * (ic + 1)],
                                                 xn[:, kc, slg], start=(kc == 0), stop=(kc == NCH - 1))
                                pe_h3[j] = kb.inc(i, "pe")

                            @kb.v
                            def _(v, e=e, ei=ei, ic=ic, tix=tix, j=j):
                                kb.w(v, "pe", pe_h3[j])
                                if j == 0:
                                    kb.w(v, "ve", ve_cwb)
                                kb.w(v, "se", se_h1[j])
                                v.scalar_tensor_tensor(sb["htmp"].ap(), ps_h[j % 2].ap(),
                                                       sb["be3_p"][:, 4 * e + ic:4 * e + ic + 1],
                                                       h1b[:, ic, 512 * tix:512 * (tix + 1)], ALU.add, ALU.mult)
                                ve_h3[j] = kb.inc(v.tensor_tensor(hsc[ei][:, ic, 512 * tix:512 * (tix + 1)],
                                                                  sb["htmp"].ap(), cwbt[:, tix, :], ALU.mult), "ve")

                    load_w3(e + 1, pe_h3[7])

                pe_y = {}
                ve_y = {}
                for dc in range(NCH):
                    for tix in range(2):
                        j = dc * 2 + tix
                        sl = slice(512 * tix, 512 * (tix + 1))

                        @kb.t
                        def _(t, grp=grp, dc=dc, tix=tix, j=j):
                            if j == 0:
                                kb.w(t, "ve")
                                kb.w(t, "wd", max(wq9["w2"][grp * NE], wq9["w2"][grp * NE + 1]))
                            if j >= 2:
                                kb.w(t, "ve", ve_y[j - 2])
                            for ei in range(NE):
                                for ic in range(4):
                                    i = t.matmul(ps_y[j % 2].ap(), we2b[ei][:, ic, 128 * dc:128 * (dc + 1)],
                                                 hsc[ei][:, ic, 512 * tix:512 * (tix + 1)],
                                                 start=(ei == 0 and ic == 0), stop=(ei == NE - 1 and ic == 3))
                            pe_y[j] = kb.inc(i, "pe")

                        @kb.v
                        def _(v, dc=dc, tix=tix, j=j, sl=sl):
                            kb.w(v, "pe", pe_y[j])
                            ve_y[j] = kb.inc(v.tensor_tensor(y_acc[:, dc, sl], y_acc[:, dc, sl],
                                                             ps_y[j % 2].ap(), ALU.add), "ve")

                load_w2(grp * NE + 2, 0, pe_y[15])
                load_w2(grp * NE + 3, 1, pe_y[15])

            pe_b2 = {}
            for dc in range(NCH):
                for tix in range(2):
                    j = dc * 2 + tix
                    sl = slice(512 * tix, 512 * (tix + 1))
                    slg = slice(th0 + 512 * tix, th0 + 512 * (tix + 1))

                    @kb.t
                    def _(t, dc=dc, tix=tix, j=j, slg=slg):
                        kb.w(t, "ve")
                        pe_b2[j] = kb.inc(t.matmul(ps_y[j % 2].ap(), sb["be2_sb"][:, 128 * dc:128 * (dc + 1)],
                                                   cw1[:, slg], start=True, stop=True), "pe")

                    @kb.v
                    def _(v, dc=dc, j=j, sl=sl):
                        kb.w(v, "pe", pe_b2[j])
                        kb.inc(v.tensor_tensor(y_acc[:, dc, sl], y_acc[:, dc, sl], ps_y[j % 2].ap(), ALU.add), "ve")

            se_g = {}
            for dc in range(NCH):
                for tix in range(2):
                    j = dc * 2 + tix
                    sl = slice(512 * tix, 512 * (tix + 1))
                    slg = slice(th0 + 512 * tix, th0 + 512 * (tix + 1))

                    @kb.s
                    def _(s, dc=dc, j=j, sl=sl):
                        kb.w(s, "ve")
                        se_g[j] = kb.inc(s.activation(h1b[:, j % 4, 0:512], y_acc[:, dc, sl], AF.Silu), "se")

                    @kb.v
                    def _(v, dc=dc, j=j, slg=slg):
                        kb.w(v, "se", se_g[j])
                        kb.inc(v.tensor_tensor(satt[:, dc, slg], satt[:, dc, slg],
                                               h1b[:, j % 4, 0:512], ALU.mult), "ve")
        st.close()

    emit_ffn1()

    @kb.y
    def _(sy):
        kb.w(sy, "ve"); kb.w(sy, "pe"); kb.w(sy, "se")
        kb.inc(sy.dma_start(x, xspill.rearrange("(c p) t -> p c t", p=128)), "sp", 16)

    @kb.v
    def _(v):
        kb.w(v, "sp")
        for kc in range(NCH):
            for ti in range(4):
                sl = slice(512 * ti, 512 * (ti + 1))
                i = v.tensor_tensor(x[:, kc, sl], x[:, kc, sl], satt[:, kc, sl], ALU.add)
        kb.inc(i, "ve")

    @kb.y
    def _(sy):
        kb.w(sy, "ve")
        kb.inc(sy.dma_start(out.rearrange("(c p) t -> p c t", p=128), x), "sp", 16)
        kb.w(sy, "sp")

    ctx.close()
    return nc, out_names


def host_in_map(inputs, b):
    f = np.float32
    A = np.ascontiguousarray

    def pcol(v):
        return A(np.asarray(v, f).reshape(NCH, 128).T)

    m = {}
    m["x_in"] = A(np.asarray(inputs["L0_attn_x"][b], f).T)
    m["wq"] = A(np.asarray(inputs["L0_attn_Wq"], f))
    m["wk0"] = A(np.asarray(inputs["L0_attn_Wk"], f))
    m["wv0"] = A(np.asarray(inputs["L0_attn_Wv"], f))
    m["wg0"] = A(np.asarray(inputs["L0_ffn_Wg"], f))
    m["we0"] = A(np.asarray(inputs["L0_ffn_We"], f))
    m["gamma_p"] = pcol(inputs["L0_attn_gamma"])
    m["beta_p"] = pcol(inputs["L0_attn_beta"])
    m["bq_p"] = pcol(inputs["L0_attn_bq"])
    m["bk_p"] = pcol(inputs["L0_attn_bk"])
    m["bv_row"] = A(np.asarray(inputs["L0_attn_bv"], f)[None, :])
    m["bg_row"] = A(np.asarray(inputs["L0_ffn_bg"], f)[None, :])
    m["be0_sb"] = A(np.asarray(inputs["L0_ffn_be"], f))
    m["wkT"] = A(np.asarray(inputs["L1_attn_Wk"], f).T)
    m["wvT"] = A(np.asarray(inputs["L1_attn_Wv"], f).T)
    m["wrT"] = A(np.asarray(inputs["L1_attn_Wr"], f).T)
    m["woT"] = A(np.asarray(inputs["L1_attn_Wo"], f).T)
    m["gwT"] = A(np.asarray(inputs["L1_ffn_gate_w"], f).T)
    m["we1"] = A(np.asarray(inputs["L1_ffn_We1"], f))
    m["we3"] = A(np.asarray(inputs["L1_ffn_We3"], f))
    m["we2"] = A(np.asarray(inputs["L1_ffn_We2"], f))
    m["ws1"] = A(np.asarray(inputs["L1_ffn_ws1"], f))
    m["ws3"] = A(np.asarray(inputs["L1_ffn_ws3"], f))
    m["ws2"] = A(np.asarray(inputs["L1_ffn_ws2"], f))
    m["td_p"] = pcol(inputs["L1_attn_time_decay"])
    m["tf_p"] = pcol(inputs["L1_attn_time_first"])
    m["tmk_p"] = pcol(np.asarray(inputs["L1_attn_time_mix_k"], f).reshape(-1))
    m["tmv_p"] = pcol(np.asarray(inputs["L1_attn_time_mix_v"], f).reshape(-1))
    m["tmr_p"] = pcol(np.asarray(inputs["L1_attn_time_mix_r"], f).reshape(-1))
    m["gb_row"] = A(np.asarray(inputs["L1_ffn_gate_b"], f)[None, :])
    be1 = np.asarray(inputs["L1_ffn_be1"], f)
    be3 = np.asarray(inputs["L1_ffn_be3"], f)
    m["be1_p"] = A(be1.reshape(E1, 4, 128).transpose(2, 0, 1).reshape(128, E1 * 4))
    m["be3_p"] = A(be3.reshape(E1, 4, 128).transpose(2, 0, 1).reshape(128, E1 * 4))
    m["be2_sb"] = A(np.asarray(inputs["L1_ffn_be2"], f))
    m["bs1_p"] = pcol(inputs["L1_ffn_bs1"])
    m["bs3_p"] = pcol(inputs["L1_ffn_bs3"])
    m["bs2_p"] = pcol(inputs["L1_ffn_bs2"])
    m["ones_col_f"] = np.ones((128, 1), f)
    m["onesrow_f"] = np.ones((1, 128), f)
    m["c1024_row"] = np.full((1, 128), 1.0 / D, f)
    m["negm_col"] = np.full((128, 1), NEG_M, f)
    m["zero_col"] = np.zeros((128, 1), f)
    m["ones_bf"] = np.ones((128, 128), f)
    m["ident_bf"] = np.eye(128, dtype=f)
    m["oh8"] = A(np.repeat(np.eye(E0, dtype=f), 128, axis=1))
    m["oh16"] = A(np.repeat(np.eye(E1, dtype=f), 128, axis=1))
    m["ident_f"] = np.eye(128, dtype=f)
    return m


_BUILT = {}


def kernel(**inputs):
    from concourse.bass_utils import run_bass_kernel_spmd
    if "nc" not in _BUILT:
        _BUILT["nc"], _ = build()
    nc = _BUILT["nc"]
    in_maps = [host_in_map(inputs, b) for b in range(8)]
    res = run_bass_kernel_spmd(nc, in_maps, core_ids=list(range(8)))
    out = np.stack([np.ascontiguousarray(r["out_fm"].T) for r in res.results], axis=0)
    return out.astype(np.float32)
